# version 07
# baseline (speedup 1.0000x reference)
import numpy as np
import ml_dtypes
from contextlib import ExitStack

import concourse.bass as bass
import concourse.mybir as mybir
from concourse.bass_utils import run_bass_kernel_spmd

F32 = mybir.dt.float32
BF16 = mybir.dt.bfloat16
I32 = mybir.dt.int32
AF = mybir.ActivationFunctionType
ALU = mybir.AluOpType
AX = mybir.AxisListType
BF = ml_dtypes.bfloat16

B, T, D = 4, 4096, 4096
HD, NH = 128, 32
TOK = 2048
NCH = 32
SP = [2, 4, 6, 8]
OWN = [[0, 3, 4, 7], [1, 2, 5, 6]]
OWNER = [0, 1, 1, 0, 0, 1, 1, 0]
POS = [OWN[OWNER[g]].index(g) for g in range(8)]
ISCALE = float(1.0 / np.sqrt(128.0))
LN_EPS = 1e-5
RMS_EPS = 1.1920929e-07
TAU_I = 10.0



def _tile_w(w):
    i, o = w.shape
    return np.ascontiguousarray(
        np.asarray(w).reshape(i // 128, 128, o // 128, 128).transpose(2, 1, 0, 3))


def _tile_sc(sc):
    sc = np.asarray(sc)
    g, w = sc.shape
    return np.ascontiguousarray(
        sc.reshape(g, w // 128, 128).transpose(1, 0, 2).reshape(w // 128, g * 128))


def _rope_tables():
    inv = 1.0 / (10000.0 ** (np.arange(0, HD, 2, dtype=np.float64) / HD))
    fr = np.arange(T, dtype=np.float64)[:, None] * inv[None, :]
    emb = np.concatenate([fr, fr], axis=-1)
    cos = np.cos(emb).astype(np.float32)
    sin = np.sin(emb).astype(np.float32)
    sgn = np.ones((HD,), np.float32)
    sgn[:64] = -1.0
    return (np.ascontiguousarray(cos.T),
            np.ascontiguousarray(sin.T * sgn[:, None]))


def prep_shared(i):
    s = {}
    s["wq0"] = _tile_w(np.asarray(i["L0_attn_wq_w"]).T)
    s["wk0"] = _tile_w(np.asarray(i["L0_attn_wk_w"]).T)
    s["wv0"] = _tile_w(np.asarray(i["L0_attn_wv_w"]).T)
    s["wo0"] = _tile_w(np.asarray(i["L0_attn_wo_w"]).T)
    s["sg0"] = _tile_w(np.asarray(i["L0_ffn_sign"]).T)
    s["qb0"] = np.asarray(i["L0_attn_wq_b"]).reshape(32, 128)
    s["kb0"] = np.asarray(i["L0_attn_wk_b"]).reshape(1, 128)
    s["vb0"] = np.asarray(i["L0_attn_wv_b"]).reshape(1, 128)
    s["ob0"] = np.asarray(i["L0_attn_wo_b"]).reshape(32, 128)
    s["fs0"] = np.asarray(i["L0_ffn_scale"]).reshape(32, 128)
    s["fb0"] = np.asarray(i["L0_ffn_bias"]).reshape(32, 128)
    s["qwq"] = _tile_w(i["L1_attn_qw_q"])
    s["qwk"] = _tile_w(i["L1_attn_qw_k"])
    s["qwv"] = _tile_w(i["L1_attn_qw_v"])
    s["qwo"] = _tile_w(i["L1_attn_qw_o"])
    s["scq"] = _tile_sc(i["L1_attn_scales_q"])
    s["sck"] = _tile_sc(i["L1_attn_scales_k"])
    s["scv"] = _tile_sc(i["L1_attn_scales_v"])
    s["sco"] = _tile_sc(i["L1_attn_scales_o"])
    s["wall"] = np.ascontiguousarray(
        np.concatenate([np.asarray(i["L1_ffn_W_res"]),
                        np.asarray(i["L1_ffn_W_pre"]),
                        np.asarray(i["L1_ffn_W_post"])], axis=0)
        .T.reshape(32, 128, 24).transpose(1, 0, 2))
    s["b24"] = np.concatenate(
        [np.asarray(i["L1_ffn_H_res_base"]).ravel(),
         np.asarray(i["L1_ffn_H_pre_base"]),
         np.asarray(i["L1_ffn_H_post_base"])]).reshape(1, 24).astype(np.float32)
    s["gate"] = np.asarray(i["L1_ffn_gate"]).reshape(1, 1).astype(np.float32)
    s["wbr"] = _tile_w(i["L1_ffn_W_branch"])
    s["identb"] = np.eye(128, dtype=np.float32).astype(BF)
    s["identf"] = np.eye(128, dtype=np.float32)
    esel = np.zeros((24, 24, 128), np.float32)
    for r in range(24):
        esel[r, r, :] = 1.0
    s["esel"] = np.ascontiguousarray(esel.transpose(1, 0, 2)
                                     .reshape(24, 24 * 128)).astype(BF)
    return s


def prep_core(i, core, cosT, sinT):
    b, p = core // 2, core % 2
    own = OWN[p]
    d = {}
    xT = np.ascontiguousarray(np.asarray(i["L0_attn_x"][b]).T)
    cols = np.concatenate([np.arange(g * 512, (g + 1) * 512) for g in own])
    d["x"] = np.ascontiguousarray(xT[:, cols])
    d["cosT"] = np.ascontiguousarray(cosT[:, cols]).astype(BF)
    d["sinT"] = np.ascontiguousarray(sinT[:, cols]).astype(BF)
    colv = np.arange(1408)[None, :]
    rowv = np.arange(128)[:, None]
    m = np.zeros((2, 128, 1408), np.float32)
    for g in range(2):
        thr = 384 if (p == 0) == (g == 0) else 896
        m[g] = np.where(colv <= thr + rowv, 0.0, -30000.0)
    d["mext"] = m.astype(BF)
    return d


def unprep_out(per_core):
    out = np.empty((B, T, D), np.float32)
    for c in range(8):
        b, p = c // 2, c % 2
        y = per_core[c]
        for lb, g in enumerate(OWN[p]):
            out[b, g * 512:(g + 1) * 512, :] = y[:, lb * 512:(lb + 1) * 512].T
    return out



class Prog:

    def __init__(self, externals=()):
        self.nc = bass.Bass("TRN2", target_bir_lowering=False)
        self.st = ExitStack()
        self.externals = set(externals)
        self.cnt = {}
        self.sems = {}
        self.dram_t = {}
        self.sb_t = {}
        for name in ["qg", "qs", "pe", "act", "dve", "cc"]:
            self.sems[name] = self.st.enter_context(self.nc.semaphore(name))
            self.cnt[name] = 0

    def dram(self, name, shape, dtype, io=None):
        if name in self.dram_t:
            return self.dram_t[name]
        kind = None
        if io == "in":
            kind = "ExternalInput"
        elif io == "out":
            kind = "ExternalOutput"
        elif io == "x":
            if ("in:" + name) in self.externals:
                kind = "ExternalInput"
            elif ("out:" + name) in self.externals:
                kind = "ExternalOutput"
        t = (self.nc.dram_tensor(name, shape, dtype, kind=kind).ap() if kind
             else self.nc.dram_tensor(name, shape, dtype).ap())
        self.dram_t[name] = t
        return t

    def sb(self, name, shape, dtype):
        if name not in self.sb_t:
            self.sb_t[name] = self.st.enter_context(
                self.nc.sbuf_tensor(name, shape, dtype))
        return self.sb_t[name]

    def sbl(self, name, shape, dtype):
        if name in self.sb_t:
            return self.sb_t[name]
        t = self._local_st.enter_context(
            self.nc.sbuf_tensor(f"{name}_p{self._phase_idx}", shape, dtype))
        self.sb_t[name] = t
        self._local_names.append(name)
        return t

    def dma(self, q, qname, out, in_):
        q.dma_start(out=out, in_=in_).then_inc(self.sems[qname], 16)
        self.cnt[qname] += 16
        return self.cnt[qname]

    def bump(self, instr, ename):
        instr.then_inc(self.sems[ename], 1)
        self.cnt[ename] += 1
        return self.cnt[ename]

    def close(self):
        self.st.close()


def section(blk, engine_name):
    return getattr(blk, engine_name)



def emit_init(P, with_x=True):
    nc = P.nc
    slot = P.sb("slot", [128, NCH * TOK], BF16)
    x = P.dram("x", [D, TOK], F32, io="in")
    xres = P.dram("xres", [D, TOK], BF16, io="x")
    mext_d = P.dram("mext", [2, 128, 1408], BF16, io="in")
    cosT_d = P.dram("cosT", [128, TOK], BF16, io="in")
    sinT_d = P.dram("sinT", [128, TOK], BF16, io="in")
    identb_d = P.dram("identb_d", [128, 128], BF16, io="in")
    identf_d = P.dram("identf_d", [128, 128], F32, io="in")
    mb = P.sb("mb", [128, 2 * 1408], BF16)
    identb = P.sb("identb", [128, 128], BF16)
    identf = P.sb("identf", [128, 128], F32)
    ones1b = P.sb("ones1b", [1, 128], BF16)
    ones1f = P.sb("ones1f", [1, 128], F32)
    onescb = P.sb("onescb", [128, 1], BF16)
    vecs = [("qb0", 32), ("ob0", 32), ("fs0", 32), ("fb0", 32),
            ("kb0", 1), ("vb0", 1)]
    for n, w in vecs:
        P.dram(n, [w, 128], F32, io="in")
        P.sb(n + "_t", [128, w], F32)
    P.dram("b24", [1, 24], F32, io="in")
    P.sb("b24_t", [24, 1], F32)
    for n, w in vecs:
        P.sbl(n + "_r", [w, 128], F32)
    P.sbl("b24_r", [1, 24], F32)

    with P.block_guard() as blk, ExitStack() as ps:
        pt = P.st_psum(ps, "pi_tr", [128, 128], F32)

        @blk.gpsimd
        def _(g):
            if with_x:
                for c in range(NCH):
                    P.dma(g, "qg", slot[:, c * TOK:(c + 1) * TOK],
                          x[c * 128:(c + 1) * 128, :])
            P.dma(g, "qg", mb[:].rearrange("p (gr w) -> p gr w", gr=2),
                  mext_d.rearrange("gr p w -> p gr w"))
            P.dma(g, "qg", identb[:], identb_d)
            P.dma(g, "qg", identf[:], identf_d)
            for n, w in vecs:
                P.dma(g, "qg", P.sb_t[n + "_r"][:], P.dram_t[n])
            P.dma(g, "qg", P.sb_t["b24_r"][:], P.dram_t["b24"])
        qg_all = P.cnt["qg"]

        @blk.vector
        def _(v):
            v.memset(ones1b[:], 1.0)
            v.memset(ones1f[:], 1.0)
            P.bump(v.memset(onescb[:], 1.0), "dve")

        for n, w in vecs + [("b24", 24)]:
            @blk.tensor
            def _(t, n=n, w=w):
                t.wait_ge(P.sems["qg"], qg_all)
                t.wait_ge(P.sems["dve"], P.cnt["dve"])
                if n == "b24":
                    P.bump(t.transpose(pt[0:24, 0:1], P.sb_t[n + "_r"][:],
                                       identf[0:1, 0:1]), "pe")
                else:
                    P.bump(t.transpose(pt[:, 0:w], P.sb_t[n + "_r"][:],
                                       identf[0:w, 0:w]), "pe")

            @blk.vector
            def _(v, n=n, w=w):
                v.wait_ge(P.sems["pe"], P.cnt["pe"])
                if n == "b24":
                    P.bump(v.tensor_copy(P.sb_t[n + "_t"][:], pt[0:24, 0:1]),
                           "dve")
                elif n == "qb0":
                    P.bump(v.tensor_scalar(P.sb_t[n + "_t"][:], pt[:, 0:w],
                                           ISCALE, None, ALU.mult), "dve")
                else:
                    P.bump(v.tensor_copy(P.sb_t[n + "_t"][:], pt[:, 0:w]),
                           "dve")

        if with_x:
            @blk.sync
            def _(s):
                s.wait_ge(P.sems["qg"], 16 * NCH)
                for c in range(NCH):
                    P.dma(s, "qs", xres[c * 128:(c + 1) * 128, :],
                          slot[:, c * TOK:(c + 1) * TOK])
                s.wait_ge(P.sems["qs"], P.cnt["qs"])


def _block_guard(self):
    return self.nc.Block()


def _st_psum(self, stack, name, shape, dtype=F32):
    return stack.enter_context(self.nc.psum_tensor(
        f"{name}_p{getattr(self, '_phase_idx', 0)}", shape, dtype))


Prog.block_guard = _block_guard
Prog.st_psum = _st_psum



def emit_proj(P, layer, mode):
    nc = P.nc
    dq = (layer == 1)
    rope = (layer == 1)
    kvh = 1 if layer == 0 else 8
    slot = P.sb_t["slot"]
    wbuf = [P.sbl("wbuf0", [128, 4096], BF16), P.sbl("wbuf1", [128, 4096], BF16)]
    qst2 = [P.sbl("qst0", [128, TOK], BF16), P.sbl("qst1", [128, TOK], BF16)]
    if mode == "oproj":
        rst = P.sbl("rst", [128, TOK], BF16)
        resk = P.sbl("resk", [128, TOK], BF16)
    if dq:
        wdq = P.sbl("wdq", [128, 4096], BF16)
        scrow = [P.sbl("scrow0", [1, 4096], BF16), P.sbl("scrow1", [1, 4096], BF16)]
    if rope and mode == "qkv":
        qtmp = P.sbl("qtmp", [128, TOK], BF16)
        cosb = P.sbl("cosb", [128, TOK], BF16)
        sinb = P.sbl("sinb", [128, TOK], BF16)
    if mode == "qkv":
        vtm = P.sbl("vtm", [128, TOK], BF16)
    ones1b = P.sb_t["ones1b"]
    identb = P.sb_t["identb"]

    if mode == "qkv":
        qd = P.dram("qd", [D, TOK], BF16, io="x")
        ko = P.dram(f"k_own{layer}", [kvh * 128, TOK], BF16, io="x")
        vo = P.dram(f"v_own{layer}", [TOK, kvh * 128], BF16, io="x")
        if layer == 0:
            wq_d = P.dram("wq0", [32, 128, 32, 128], F32, io="in")
            wk_d = P.dram("wk0", [1, 128, 32, 128], F32, io="in")
            wv_d = P.dram("wv0", [1, 128, 32, 128], F32, io="in")
            tasks = ([("q", oc, wq_d, None, qd, oc * 128) for oc in range(32)]
                     + [("k", 0, wk_d, None, ko, 0),
                        ("v", 0, wv_d, None, vo, 0)])
        else:
            wq_d = P.dram("qwq", [32, 128, 32, 128], I32, io="in")
            wk_d = P.dram("qwk", [8, 128, 32, 128], I32, io="in")
            wv_d = P.dram("qwv", [8, 128, 32, 128], I32, io="in")
            scq_d = P.dram("scq", [32, 4096], F32, io="in")
            sck_d = P.dram("sck", [8, 4096], F32, io="in")
            scv_d = P.dram("scv", [8, 4096], F32, io="in")
            tasks = ([("q", oc, wq_d, scq_d, qd, oc * 128) for oc in range(32)]
                     + [("k", oc, wk_d, sck_d, ko, oc * 128) for oc in range(8)]
                     + [("v", oc, wv_d, scv_d, vo, oc * 128) for oc in range(8)])
    else:
        rbuf = P.dram("rbuf", [D, TOK], BF16, io="x")
        xres = P.dram("xres", [D, TOK], BF16, io="x")
        if layer == 0:
            wo_d = P.dram("wo0", [32, 128, 32, 128], F32, io="in")
            tasks = [("o", oc, wo_d, None, rbuf, oc * 128) for oc in range(32)]
        else:
            wo_d = P.dram("qwo", [32, 128, 32, 128], I32, io="in")
            sco_d = P.dram("sco", [32, 4096], F32, io="in")
            tasks = [("o", oc, wo_d, sco_d, rbuf, oc * 128) for oc in range(32)]

    with P.block_guard() as blk, ExitStack() as ps:
        pp = P.st_psum(ps, f"pp_{mode}{layer}", [128, TOK], F32)
        if dq:
            pbc = P.st_psum(ps, f"pbc_{mode}{layer}", [128, 512], F32)
        if mode == "qkv":
            ptr = P.st_psum(ps, f"ptr_{mode}{layer}", [128, TOK], BF16)

        if rope and mode == "qkv":
            @blk.gpsimd
            def _(g):
                P.dma(g, "qg", cosb[:], P.dram_t["cosT"])
                P.dma(g, "qg", sinb[:], P.dram_t["sinT"])

        mm_done = {}
        vtm_done = {}
        tr_done = {}
        ev_done = {}
        st_done = {}
        ep_done = {}
        ld_done = {}

        for i, (kind, oc, wd, scd, dst, row) in enumerate(tasks):
            @blk.gpsimd
            def _(g, i=i):
                def load(j):
                    kj, ocj, wdj, scdj, _d, _r = tasks[j]
                    if j >= 2 and (j - 2) in mm_done:
                        g.wait_ge(P.sems["pe"], mm_done[j - 2])
                    P.dma(g, "qg", wbuf[j % 2][:],
                          wdj[ocj].rearrange("ip ic op -> ip (ic op)"))
                    if dq:
                        P.dma(g, "qg", scrow[j % 2][:],
                              scdj[ocj].unsqueeze(0))
                    ld_done[j] = P.cnt["qg"]
                if i == 0:
                    load(0)
                    if len(tasks) > 1:
                        load(1)
                elif i + 1 < len(tasks):
                    load(i + 1)

            if mode == "oproj":
                @blk.sync
                def _(s, i=i, row=row):
                    if i >= 1:
                        s.wait_ge(P.sems["dve"], ep_done[i - 1])
                    P.dma(s, "qs", resk[:], xres[row:row + 128, :])
                    ld_done[("r", i)] = P.cnt["qs"]

            if dq:
                dve0 = P.cnt["dve"]
                pe0 = P.cnt["pe"]

                @blk.tensor
                def _(t, i=i, dve0=dve0):
                    t.wait_ge(P.sems["qg"], ld_done[i])
                    if i >= 1:
                        t.wait_ge(P.sems["act"], ev_done[i - 1])
                    for s in range(8):
                        if s >= 1:
                            t.wait_ge(P.sems["dve"], dve0 + s)
                        P.bump(t.matmul(pbc[:], ones1b[:],
                                        scrow[i % 2][0:1, s * 512:(s + 1) * 512],
                                        start=True, stop=True), "pe")

                @blk.vector
                def _(v, i=i, pe0=pe0):
                    if i >= 1:
                        v.wait_ge(P.sems["pe"], mm_done[i - 1])
                    for s in range(8):
                        v.wait_ge(P.sems["pe"], pe0 + s + 1)
                        P.bump(v.scalar_tensor_tensor(
                            wdq[:, s * 512:(s + 1) * 512],
                            wbuf[i % 2][:, s * 512:(s + 1) * 512],
                            -8.0, pbc[:], ALU.add, ALU.mult), "dve")
                dq_dve_done = P.cnt["dve"]

            @blk.tensor
            def _(t, i=i):
                src = wdq if dq else wbuf[i % 2]
                if not dq:
                    t.wait_ge(P.sems["qg"], ld_done[i])
                else:
                    t.wait_ge(P.sems["dve"], dq_dve_done)
                if i >= 1:
                    t.wait_ge(P.sems["act"], ev_done[i - 1])
                for ic in range(NCH):
                    for tt in range(4):
                        ins = t.matmul(
                            pp[:, tt * 512:(tt + 1) * 512],
                            src[:, ic * 128:(ic + 1) * 128],
                            slot[:, ic * TOK + tt * 512:ic * TOK + (tt + 1) * 512],
                            start=(ic == 0), stop=(ic == NCH - 1))
                        if ic == NCH - 1 and tt == 3:
                            P.bump(ins, "pe")
                mm_done[i] = P.cnt["pe"]

            @blk.scalar
            def _(sc, i=i, kind=kind, oc=oc):
                qst = qst2[i % 2]
                sc.wait_ge(P.sems["pe"], mm_done[i])
                if i >= 2:
                    sc.wait_ge(P.sems["qs"], st_done[i - 2])
                if mode == "qkv":
                    if kind == "q":
                        bias = (P.sb_t["qb0_t"][:, oc:oc + 1] if layer == 0
                                else 0.0)
                        P.bump(sc.activation(qst[:], pp[:], AF.Identity,
                                             bias=bias, scale=ISCALE), "act")
                    elif kind == "k":
                        bias = P.sb_t["kb0_t"][:] if layer == 0 else 0.0
                        P.bump(sc.activation(qst[:], pp[:], AF.Identity,
                                             bias=bias, scale=1.0), "act")
                    else:
                        bias = P.sb_t["vb0_t"][:] if layer == 0 else 0.0
                        P.bump(sc.activation(qst[:], pp[:], AF.Identity,
                                             bias=bias, scale=1.0), "act")
                else:
                    if layer == 0:
                        P.bump(sc.activation(qst[:], pp[:], AF.Identity,
                                             bias=P.sb_t["ob0_t"][:, oc:oc + 1],
                                             scale=1.0), "act")
                    else:
                        P.bump(sc.copy(qst[:], pp[:]), "act")
                ev_done[i] = P.cnt["act"]

            if mode == "qkv" and kind in ("q", "k") and rope:
                @blk.vector
                def _(v, i=i):
                    qst = qst2[i % 2]
                    v.wait_ge(P.sems["act"], ev_done[i])
                    v.tensor_copy(qtmp[0:64, :], qst[64:128, :])
                    v.tensor_copy(qtmp[64:128, :], qst[0:64, :])
                    v.tensor_tensor(qtmp[:], qtmp[:], P.sb_t["sinb"][:], ALU.mult)
                    v.tensor_tensor(qst[:], qst[:], P.sb_t["cosb"][:], ALU.mult)
                    P.bump(v.tensor_tensor(qst[:], qst[:], qtmp[:], ALU.add),
                           "dve")
                    ep_done[i] = P.cnt["dve"]
            elif mode == "qkv" and kind == "v":
                @blk.tensor
                def _(t, i=i):
                    qst = qst2[i % 2]
                    t.wait_ge(P.sems["act"], ev_done[i])
                    if i >= 1 and tasks[i - 1][0] == "v":
                        t.wait_ge(P.sems["dve"], vtm_done[i - 1])
                    for c in range(16):
                        ins = t.transpose(ptr[:, c * 128:(c + 1) * 128],
                                          qst[:, c * 128:(c + 1) * 128],
                                          identb[:])
                        if c == 15:
                            P.bump(ins, "pe")
                    tr_done[i] = P.cnt["pe"]

                @blk.vector
                def _(v, i=i):
                    v.wait_ge(P.sems["pe"], tr_done[i])
                    if i >= 1 and tasks[i - 1][0] == "v":
                        v.wait_ge(P.sems["qs"], st_done[i - 1])
                    P.bump(v.tensor_copy(vtm[:], ptr[:]), "dve")
                    vtm_done[i] = P.cnt["dve"]
            elif mode == "oproj":
                @blk.vector
                def _(v, i=i):
                    v.wait_ge(P.sems["act"], ev_done[i])
                    v.wait_ge(P.sems["qs"], ld_done[("r", i)])
                    if i >= 1:
                        v.wait_ge(P.sems["qs"], st_done[i - 1])
                    P.bump(v.tensor_tensor(rst[:], qst2[i % 2][:], resk[:],
                                           ALU.add), "dve")
                    ep_done[i] = P.cnt["dve"]

            @blk.sync
            def _(s, i=i, kind=kind, dst=dst, row=row):
                if mode == "qkv" and kind == "v":
                    s.wait_ge(P.sems["dve"], vtm_done[i])
                    P.dma(s, "qs", dst[:, row:row + 128].rearrange(
                        "(c p) hd -> p c hd", p=128),
                        vtm[:].rearrange("p (c hd) -> p c hd", c=16))
                elif mode == "qkv":
                    if rope and kind in ("q", "k"):
                        s.wait_ge(P.sems["dve"], ep_done[i])
                    else:
                        s.wait_ge(P.sems["act"], ev_done[i])
                    P.dma(s, "qs", dst[row:row + 128, :], qst2[i % 2][:])
                else:
                    s.wait_ge(P.sems["dve"], ep_done[i])
                    P.dma(s, "qs", dst[row:row + 128, :], rst[:])
                st_done[i] = P.cnt["qs"]

        @blk.sync
        def _(s):
            s.wait_ge(P.sems["qs"], P.cnt["qs"])



def emit_gather(P, layer):
    kvh = 1 if layer == 0 else 8
    ko = P.dram(f"k_own{layer}", [kvh * 128, TOK], BF16, io="x")
    vo = P.dram(f"v_own{layer}", [TOK, kvh * 128], BF16, io="x")
    kf = P.dram(f"k_full{layer}", [2 * kvh * 128, TOK], BF16, io="x")
    vf = P.dram(f"v_full{layer}", [2 * TOK, kvh * 128], BF16, io="x")
    groups = [[0, 1], [2, 3], [4, 5], [6, 7]]
    with P.block_guard() as blk:
        @blk.gpsimd
        def _(g):
            g.collective_compute(
                "AllGather", ALU.bypass, replica_groups=groups,
                ins=[ko.opt()], outs=[kf.opt()]).then_inc(P.sems["cc"], 1)
            P.cnt["cc"] += 1
            g.collective_compute(
                "AllGather", ALU.bypass, replica_groups=groups,
                ins=[vo.opt()], outs=[vf.opt()]).then_inc(P.sems["cc"], 1)
            P.cnt["cc"] += 1
            g.wait_ge(P.sems["cc"], P.cnt["cc"])



def emit_attn(P, layer):
    nc = P.nc
    kvh = 1 if layer == 0 else 8
    qd = P.dram("qd", [D, TOK], BF16, io="x")
    kf = P.dram(f"k_full{layer}", [2 * kvh * 128, TOK], BF16, io="x")
    vf = P.dram(f"v_full{layer}", [2 * TOK, kvh * 128], BF16, io="x")
    slot = P.sb_t["slot"]
    mb = P.sb_t["mb"]
    identb = P.sb_t["identb"]
    kbuf = P.sbl("kbuf", [128, T], BF16)
    vbuf = P.sbl("vbuf", [128, 32 * 128], BF16)
    qbuf = [P.sbl("qbuf0", [128, TOK], BF16), P.sbl("qbuf1", [128, TOK], BF16)]
    pbufs = [P.sbl("pbuf0", [128, 1024], BF16), P.sbl("pbuf1", [128, 1024], BF16)]
    ptb = [P.sbl("ptb0", [128, 1024], BF16), P.sbl("ptb1", [128, 1024], BF16)]
    nmb = [P.sbl("nmb0", [128, 1], F32), P.sbl("nmb1", [128, 1], F32)]
    mtmp = [P.sbl("mtmp0", [128, 1], F32), P.sbl("mtmp1", [128, 1], F32)]
    lrun = [P.sbl("lrun0", [128, 1], F32), P.sbl("lrun1", [128, 1], F32)]
    lpart = [P.sbl("lpart0", [128, 1], F32), P.sbl("lpart1", [128, 1], F32)]
    linv = [P.sbl("linv0", [128, 1], F32), P.sbl("linv1", [128, 1], F32)]
    ctm = [P.sbl("ctm0", [128, 128], BF16), P.sbl("ctm1", [128, 128], BF16)]

    with P.block_guard() as blk, ExitStack() as ps:
        psc = [P.st_psum(ps, f"psc{j}_{layer}", [128, 1024], F32)
               for j in range(2)]
        ppt = [P.st_psum(ps, f"ppt{j}_{layer}", [128, 1024], BF16)
               for j in range(2)]
        bankT = [P.st_psum(ps, f"bkt{j}_{layer}", [128, 1024], BF16)
                 for j in range(2)]
        pav = [bankT[j][:, 0:256].bitcast(F32) for j in range(2)]
        pct = [bankT[j][:, 512:640] for j in range(2)]

        kv_ld = {}
        q_ld = {}
        heads = [(h, h // (NH // kvh)) for h in range(NH)]
        pr = {"psc_exp": [0, 0],
              "pav_rd": [0, 0],
              "pct_rd": [0, 0],
              "nm_rd": [0, 0],
              "stat_rd": [0, 0],
              "unit": 0,
              "ppt_rd": [0, 0],
              "qk_tail": {}}

        def emit_kv_load(g, kv):
            for gblk in range(8):
                r, pos = OWNER[gblk], POS[gblk]
                P.dma(g, "qs",
                      kbuf[:, gblk * 512:(gblk + 1) * 512],
                      kf[r * kvh * 128 + kv * 128:
                         r * kvh * 128 + (kv + 1) * 128,
                         pos * 512:(pos + 1) * 512])
                P.dma(g, "qs",
                      vbuf[:].rearrange("p (c hd) -> p c hd", c=32)
                      [:, gblk * 4:(gblk + 1) * 4, :],
                      vf[r * TOK + pos * 512: r * TOK + (pos + 1) * 512,
                         kv * 128:(kv + 1) * 128]
                      .rearrange("(c p) hd -> p c hd", p=128))
            kv_ld[kv] = P.cnt["qs"]

        @blk.sync
        def _(g):
            emit_kv_load(g, 0)

        @blk.sync
        def _(s):
            P.dma(s, "qs", qbuf[0][:], qd[heads[0][0] * 128:
                                          heads[0][0] * 128 + 128, :])
            q_ld[0] = P.cnt["qs"]

        for hi, (h, kv) in enumerate(heads):
            @blk.sync
            def _(s, hi=hi):
                if hi + 1 < NH:
                    if hi >= 1:
                        s.wait_ge(P.sems["pe"], pr["qk_tail"][hi - 1])
                    hn = heads[hi + 1][0]
                    P.dma(s, "qs", qbuf[(hi + 1) % 2][:],
                          qd[hn * 128:hn * 128 + 128, :])
                    q_ld[hi + 1] = P.cnt["qs"]

            for lb in range(4):
                for qc in range(4):
                    up = pr["unit"] % 2
                    pr["unit"] += 1
                    emit_attn_qc(P, blk, hi, kv, h, lb, qc, SP[lb] // 2,
                                 psc, ppt, pav[up], pct[up], kbuf, vbuf,
                                 qbuf, pbufs, ptb, nmb[up], mtmp[up],
                                 lrun[up], lpart[up], linv[up], ctm[up], mb,
                                 identb, slot, kv_ld, q_ld, pr, up,
                                 pr["unit"] - 1)
            pr["qk_tail"][hi] = P.cnt["pe"]

            if hi + 1 < NH and heads[hi + 1][1] != kv:
                @blk.sync
                def _(g, kv=kv):
                    g.wait_ge(P.sems["pe"], P.cnt["pe"])
                    emit_kv_load(g, kv + 1)

        @blk.vector
        def _(v):
            v.wait_ge(P.sems["dve"], P.cnt["dve"])
            v.nop()


def emit_attn_qc(P, blk, hi, kv, h, lb, qc, nkb, psc, ppt, pav, pct,
                 kbuf, vbuf, qbuf, pbufs, ptb, nmb, mtmp, lrun, lpart, linv,
                 ctm, mb, identb, slot, kv_ld, q_ld, pr, up, uo):
    kbx = lambda kb: (kb + uo) % 2
    qsl = qbuf[hi % 2][:, lb * 512 + qc * 128: lb * 512 + qc * 128 + 128]
    mext_sl = mb[:].rearrange("p (gr w) -> p gr w", gr=2)[
        :, lb % 2, 384 - 128 * qc: 384 - 128 * qc + 1024]

    p1_mm = {}
    p1_red = {}
    for kb in range(nkb):
        @blk.tensor
        def _(t, kb=kb):
            if kb == 0:
                t.wait_ge(P.sems["qs"], kv_ld[kv])
                t.wait_ge(P.sems["qs"], q_ld[hi])
            if kb < 2:
                t.wait_ge(P.sems["act"], pr["psc_exp"][kbx(kb)])
            else:
                t.wait_ge(P.sems["dve"], p1_red[kb - 2])
            ins = None
            for half in range(2):
                ins = t.matmul(psc[kbx(kb)][:, half * 512:(half + 1) * 512],
                               qsl,
                               kbuf[:, kb * 1024 + half * 512:
                                    kb * 1024 + (half + 1) * 512],
                               start=True, stop=True)
            P.bump(ins, "pe")
            p1_mm[kb] = P.cnt["pe"]

        @blk.vector
        def _(v, kb=kb):
            v.wait_ge(P.sems["pe"], p1_mm[kb])
            if kb == 0:
                v.wait_ge(P.sems["act"], pr["nm_rd"][up])
            if kb == nkb - 1:
                v.tensor_tensor(psc[kbx(kb)][:], psc[kbx(kb)][:], mext_sl,
                                ALU.add)
            v.tensor_reduce(lpart[:], psc[kbx(kb)][:, 0:512], AX.X, ALU.max,
                            negate=True)
            v.tensor_reduce(mtmp[:], psc[kbx(kb)][:, 512:1024], AX.X, ALU.max,
                            negate=True)
            v.tensor_tensor(lpart[:], lpart[:], mtmp[:], ALU.min)
            if kb == 0:
                P.bump(v.tensor_copy(nmb[:], lpart[:]), "dve")
            else:
                P.bump(v.tensor_tensor(nmb[:], nmb[:], lpart[:], ALU.min),
                       "dve")
            p1_red[kb] = P.cnt["dve"]
    nm_done = P.cnt["dve"]

    p2_mm = {}
    p2_msk = {}
    p2_exp = {}
    p2_tr = {}
    p2_ev = {}
    av_done = {}
    l_upd = {}
    for kb in range(nkb):
        @blk.tensor
        def _(t, kb=kb):
            if kb == 0:
                t.wait_ge(P.sems["dve"], nm_done)
            if kb >= 2:
                t.wait_ge(P.sems["act"], p2_exp[kb - 2])
            ins = None
            for half in range(2):
                ins = t.matmul(psc[kbx(kb)][:, half * 512:(half + 1) * 512],
                               qsl,
                               kbuf[:, kb * 1024 + half * 512:
                                    kb * 1024 + (half + 1) * 512],
                               start=True, stop=True)
            P.bump(ins, "pe")
            p2_mm[kb] = P.cnt["pe"]

        if kb == nkb - 1:
            @blk.vector
            def _(v, kb=kb):
                v.wait_ge(P.sems["pe"], p2_mm[kb])
                P.bump(v.tensor_tensor(psc[kbx(kb)][:], psc[kbx(kb)][:],
                                       mext_sl, ALU.add), "dve")
                p2_msk[kb] = P.cnt["dve"]

        @blk.scalar
        def _(sc, kb=kb):
            if kb in p2_msk:
                sc.wait_ge(P.sems["dve"], p2_msk[kb])
            else:
                sc.wait_ge(P.sems["pe"], p2_mm[kb])
            if kb >= 2:
                sc.wait_ge(P.sems["pe"], p2_tr[kb - 2])
            if kb >= 1:
                sc.wait_ge(P.sems["dve"], l_upd[kb - 1])
            P.bump(sc.activation(pbufs[kbx(kb)][:], psc[kbx(kb)][:], AF.Exp,
                                 bias=nmb[:], scale=1.0,
                                 accum_out=lpart[:]), "act")
            p2_exp[kb] = P.cnt["act"]
            pr["psc_exp"][kbx(kb)] = P.cnt["act"]

        @blk.vector
        def _(v, kb=kb):
            v.wait_ge(P.sems["act"], p2_exp[kb])
            if kb == 0:
                P.bump(v.tensor_copy(lrun[:], lpart[:]), "dve")
            else:
                P.bump(v.tensor_tensor(lrun[:], lrun[:], lpart[:], ALU.add),
                       "dve")
            l_upd[kb] = P.cnt["dve"]

        @blk.tensor
        def _(t, kb=kb):
            t.wait_ge(P.sems["act"], p2_exp[kb])
            if kb >= 2:
                t.wait_ge(P.sems["dve"], p2_ev[kb - 2])
            else:
                t.wait_ge(P.sems["dve"], pr["ppt_rd"][kbx(kb)])
            ins = None
            for c in range(8):
                ins = t.transpose(ppt[kbx(kb)][:, c * 128:(c + 1) * 128],
                                  pbufs[kbx(kb)][:, c * 128:(c + 1) * 128],
                                  identb[:])
            P.bump(ins, "pe")
            p2_tr[kb] = P.cnt["pe"]

        @blk.vector
        def _(v, kb=kb):
            v.wait_ge(P.sems["pe"], p2_tr[kb])
            if kb >= 2:
                v.wait_ge(P.sems["pe"], av_done[kb - 2])
            P.bump(v.tensor_copy(ptb[kbx(kb)][:], ppt[kbx(kb)][:]), "dve")
            p2_ev[kb] = P.cnt["dve"]
            pr["ppt_rd"][kbx(kb)] = P.cnt["dve"]

        @blk.tensor
        def _(t, kb=kb):
            t.wait_ge(P.sems["dve"], p2_ev[kb])
            if kb == 0:
                t.wait_ge(P.sems["dve"], pr["pav_rd"][up])
                t.wait_ge(P.sems["dve"], pr["pct_rd"][up])
            ins = None
            for c in range(8):
                ins = t.matmul(pav,
                               ptb[kbx(kb)][:, c * 128:(c + 1) * 128],
                               vbuf[:, (kb * 8 + c) * 128:
                                    (kb * 8 + c + 1) * 128],
                               start=(kb == 0 and c == 0),
                               stop=(kb == nkb - 1 and c == 7))
            P.bump(ins, "pe")
            av_done[kb] = P.cnt["pe"]
    av_all = P.cnt["pe"]
    l_all = P.cnt["dve"]

    @blk.scalar
    def _(sc):
        sc.wait_ge(P.sems["dve"], l_all)
        sc.wait_ge(P.sems["dve"], pr["stat_rd"][up])
        sc.activation(linv[:], lrun[:], AF.Ln, bias=0.0, scale=1.0)
        P.bump(sc.activation(linv[:], linv[:], AF.Exp, bias=0.0,
                             scale=-1.0), "act")
    linv_done = P.cnt["act"]
    pr["nm_rd"][up] = P.cnt["act"]

    @blk.vector
    def _(v):
        v.wait_ge(P.sems["pe"], av_all)
        v.wait_ge(P.sems["act"], linv_done)
        P.bump(v.tensor_scalar(ctm[:], pav, linv[:], None, ALU.mult),
               "dve")
    ctm_done = P.cnt["dve"]
    pr["pav_rd"][up] = ctm_done
    pr["stat_rd"][up] = ctm_done

    @blk.tensor
    def _(t):
        t.wait_ge(P.sems["dve"], ctm_done)
        t.wait_ge(P.sems["dve"], pr["pct_rd"][up])
        P.bump(t.transpose(pct, ctm[:], identb[:]), "pe")
    ctr_done = P.cnt["pe"]

    @blk.vector
    def _(v):
        v.wait_ge(P.sems["pe"], ctr_done)
        P.bump(v.tensor_copy(
            slot[:, h * TOK + lb * 512 + qc * 128:
                 h * TOK + lb * 512 + qc * 128 + 128], pct), "dve")
    pr["pct_rd"][up] = P.cnt["dve"]



def emit_ln(P, final=False):
    nc = P.nc
    rbuf = P.dram("rbuf", [D, TOK], BF16, io="x")
    slot = P.sb_t["slot"]
    onescb = P.sb_t["onescb"]
    ones1f = P.sb_t["ones1f"]
    rck = [P.sbl("rck0", [128, TOK], BF16), P.sbl("rck1", [128, TOK], BF16)]
    rsq = P.sbl("rsq", [128, TOK], BF16)
    arow = P.sbl("arow", [1, TOK], F32)
    brow = P.sbl("brow", [1, TOK], F32)
    crow = P.sbl("crow", [1, TOK], F32)
    abuf = P.sbl("abuf", [128, TOK], BF16)
    bbuf = P.sbl("bbuf", [128, TOK], BF16)
    if final:
        yout = P.dram("y", [D, TOK], F32, io="out")
        yst = [P.sbl("yst0", [128, TOK], F32), P.sbl("yst1", [128, TOK], F32)]
    else:
        xres = P.dram("xres", [D, TOK], BF16, io="x")

    with P.block_guard() as blk, ExitStack() as ps:
        psum_s = P.st_psum(ps, "ps_lnsum", [1, TOK], F32)
        psum_q = P.st_psum(ps, "ps_lnsq", [1, TOK], F32)
        ld = {}
        sq = {}
        mm_done = {}
        for c in range(NCH):
            @blk.sync
            def _(s, c=c):
                if c >= 2:
                    s.wait_ge(P.sems["pe"], mm_done[c - 2])
                P.dma(s, "qs", rck[c % 2][:], rbuf[c * 128:(c + 1) * 128, :])
                ld[c] = P.cnt["qs"]

            @blk.vector
            def _(v, c=c):
                v.wait_ge(P.sems["qs"], ld[c])
                if c >= 1:
                    v.wait_ge(P.sems["pe"], mm_done[c - 1])
                P.bump(v.tensor_tensor(rsq[:], rck[c % 2][:], rck[c % 2][:],
                                       ALU.mult), "dve")
                sq[c] = P.cnt["dve"]

            @blk.tensor
            def _(t, c=c):
                t.wait_ge(P.sems["dve"], sq[c])
                ins = None
                for tt in range(4):
                    t.matmul(psum_s[0:1, tt * 512:(tt + 1) * 512], onescb[:],
                             rck[c % 2][:, tt * 512:(tt + 1) * 512],
                             start=(c == 0), stop=(c == NCH - 1))
                    ins = t.matmul(psum_q[0:1, tt * 512:(tt + 1) * 512],
                                   onescb[:], rsq[:, tt * 512:(tt + 1) * 512],
                                   start=(c == 0), stop=(c == NCH - 1))
                P.bump(ins, "pe")
                mm_done[c] = P.cnt["pe"]
        stats_done = P.cnt["pe"]

        @blk.vector
        def _(v):
            v.wait_ge(P.sems["pe"], stats_done)
            v.tensor_scalar(arow[:], psum_s[:], 1.0 / D, None, ALU.mult)
            v.tensor_scalar(brow[:], psum_q[:], 1.0 / D, None, ALU.mult)
            v.tensor_tensor(crow[:], arow[:], arow[:], ALU.mult)
            P.bump(v.tensor_tensor(brow[:], brow[:], crow[:], ALU.subtract),
                   "dve")
        var_done = P.cnt["dve"]

        @blk.vector
        def _(v):
            P.bump(v.tensor_scalar(brow[:], brow[:], float(LN_EPS), None,
                                   ALU.add), "dve")
        vare_done = P.cnt["dve"]

        @blk.scalar
        def _(sc):
            sc.wait_ge(P.sems["dve"], vare_done)
            sc.activation(brow[:], brow[:], AF.Ln, bias=0.0, scale=1.0)
            P.bump(sc.activation(brow[:], brow[:], AF.Exp, bias=0.0,
                                 scale=-0.5), "act")
        rstd_done = P.cnt["act"]

        @blk.vector
        def _(v):
            v.wait_ge(P.sems["act"], rstd_done)
            v.tensor_tensor(arow[:], arow[:], brow[:], ALU.mult)
            P.bump(v.tensor_scalar(arow[:], arow[:], -1.0, None, ALU.mult),
                   "dve")
        ab_done = P.cnt["dve"]

    with P.block_guard() as blk, ExitStack() as ps:
        pb = [P.st_psum(ps, f"ps_lnb{j}", [128, 512], F32) for j in range(2)]
        bc_done = {}
        bcmm = {}
        for tt in range(4):
            @blk.tensor
            def _(t, tt=tt):
                if tt == 0:
                    t.wait_ge(P.sems["dve"], ab_done)
                else:
                    t.wait_ge(P.sems["dve"], bc_done[tt - 1])
                t.matmul(pb[0][:], ones1f[:],
                         brow[0:1, tt * 512:(tt + 1) * 512],
                         start=True, stop=True)
                P.bump(t.matmul(pb[1][:], ones1f[:],
                                arow[0:1, tt * 512:(tt + 1) * 512],
                                start=True, stop=True), "pe")
                bcmm[tt] = P.cnt["pe"]

            @blk.vector
            def _(v, tt=tt):
                v.wait_ge(P.sems["pe"], bcmm[tt])
                v.tensor_copy(abuf[:, tt * 512:(tt + 1) * 512], pb[0][:])
                P.bump(v.tensor_copy(bbuf[:, tt * 512:(tt + 1) * 512],
                                     pb[1][:]), "dve")
                bc_done[tt] = P.cnt["dve"]
        bc_all = P.cnt["dve"]

        ld = {}
        ap_done = {}
        st_done = {}
        for c in range(NCH):
            @blk.sync
            def _(s, c=c):
                if c >= 2:
                    s.wait_ge(P.sems["dve"], ap_done[c - 2])
                P.dma(s, "qs", rck[c % 2][:], rbuf[c * 128:(c + 1) * 128, :])
                ld[c] = P.cnt["qs"]

            @blk.vector
            def _(v, c=c):
                v.wait_ge(P.sems["qs"], ld[c])
                if c == 0:
                    v.wait_ge(P.sems["dve"], bc_all)
                if final:
                    if c >= 2:
                        v.wait_ge(P.sems["qs"], st_done[c - 2])
                    v.tensor_tensor(rsq[:], rck[c % 2][:], abuf[:], ALU.mult)
                    P.bump(v.tensor_tensor(yst[c % 2][:], rsq[:], bbuf[:],
                                           ALU.add), "dve")
                else:
                    v.tensor_tensor(rsq[:], rck[c % 2][:], abuf[:], ALU.mult)
                    P.bump(v.tensor_tensor(
                        slot[:, c * TOK:(c + 1) * TOK], rsq[:], bbuf[:],
                        ALU.add), "dve")
                ap_done[c] = P.cnt["dve"]

            @blk.sync
            def _(s, c=c):
                s.wait_ge(P.sems["dve"], ap_done[c])
                if final:
                    P.dma(s, "qs", P.dram_t["y"][c * 128:(c + 1) * 128, :],
                          yst[c % 2][:])
                else:
                    P.dma(s, "qs",
                          P.dram_t["xres"][c * 128:(c + 1) * 128, :],
                          slot[:, c * TOK:(c + 1) * TOK])
                st_done[c] = P.cnt["qs"]

        @blk.sync
        def _(s):
            s.wait_ge(P.sems["qs"], P.cnt["qs"])



def emit_ffn0(P):
    nc = P.nc
    slot = P.sb_t["slot"]
    rbuf = P.dram("rbuf", [D, TOK], BF16, io="x")
    sg_d = P.dram("sg0", [32, 128, 32, 128], F32, io="in")
    wbuf = [P.sbl("wbuf0", [128, 4096], BF16), P.sbl("wbuf1", [128, 4096], BF16)]
    qst = P.sbl("qst", [128, TOK], BF16)
    rst = P.sbl("rst", [128, TOK], BF16)

    with P.block_guard() as blk, ExitStack() as ps:
        pp = P.st_psum(ps, "pp_ffn0", [128, TOK], F32)
        ld, mm_done, ev_done, ep_done, st_done = {}, {}, {}, {}, {}
        for i in range(32):
            @blk.gpsimd
            def _(g, i=i):
                def load(j):
                    if j >= 2:
                        g.wait_ge(P.sems["pe"], mm_done[j - 2])
                    P.dma(g, "qg", wbuf[j % 2][:],
                          sg_d[j].rearrange("ip ic op -> ip (ic op)"))
                    ld[j] = P.cnt["qg"]
                if i == 0:
                    load(0)
                    load(1)
                elif i + 1 < 32:
                    load(i + 1)

            @blk.tensor
            def _(t, i=i):
                t.wait_ge(P.sems["qg"], ld[i])
                if i >= 1:
                    t.wait_ge(P.sems["act"], ev_done[i - 1])
                for ic in range(NCH):
                    for tt in range(4):
                        ins = t.matmul(
                            pp[:, tt * 512:(tt + 1) * 512],
                            wbuf[i % 2][:, ic * 128:(ic + 1) * 128],
                            slot[:, ic * TOK + tt * 512:ic * TOK + (tt + 1) * 512],
                            start=(ic == 0), stop=(ic == NCH - 1))
                P.bump(ins, "pe")
                mm_done[i] = P.cnt["pe"]

            @blk.scalar
            def _(sc, i=i):
                sc.wait_ge(P.sems["pe"], mm_done[i])
                if i >= 1:
                    sc.wait_ge(P.sems["dve"], ep_done[i - 1])
                P.bump(sc.activation(qst[:], pp[:], AF.Identity,
                                     bias=P.sb_t["fb0_t"][:, i:i + 1],
                                     scale=P.sb_t["fs0_t"][:, i:i + 1]), "act")
                ev_done[i] = P.cnt["act"]

            @blk.vector
            def _(v, i=i):
                v.wait_ge(P.sems["act"], ev_done[i])
                if i >= 1:
                    v.wait_ge(P.sems["qs"], st_done[i - 1])
                P.bump(v.tensor_tensor(rst[:], qst[:],
                                       slot[:, i * TOK:(i + 1) * TOK],
                                       ALU.add), "dve")
                ep_done[i] = P.cnt["dve"]

            @blk.sync
            def _(s, i=i):
                s.wait_ge(P.sems["dve"], ep_done[i])
                P.dma(s, "qs", rbuf[i * 128:(i + 1) * 128, :], rst[:])
                st_done[i] = P.cnt["qs"]

        @blk.sync
        def _(s):
            s.wait_ge(P.sems["qs"], P.cnt["qs"])



def emit_ffn1(P):
    nc = P.nc
    slot = P.sb_t["slot"]
    rbuf = P.dram("rbuf", [D, TOK], BF16, io="x")
    wall_d = P.dram("wall", [128, 32, 24], F32, io="in")
    gate_d = P.dram("gate", [1, 1], F32, io="in")
    wbr_d = P.dram("wbr", [8, 128, 8, 128], F32, io="in")
    esel_d = P.dram("esel", [24, 24 * 128], BF16, io="in")
    ones1f = P.sb_t["ones1f"]
    ones1b = P.sb_t["ones1b"]
    onescb = P.sb_t["onescb"]
    identf = P.sb_t["identf"]
    b24t = P.sb_t["b24_t"]

    wallb = P.sbl("wallb", [128, 32 * 24], BF16)
    wbrw = [P.sbl("wbrw0", [128, 8 * 128], BF16),
            P.sbl("wbrw1", [128, 8 * 128], BF16)]
    gates = P.sbl("gates", [1, 8], F32)
    g128 = P.sbl("g128", [128, 1], F32)
    gm1 = P.sbl("gm1", [128, 1], F32)
    rsq = P.sbl("rsq", [128, TOK], BF16)
    srow = P.sbl("srow", [1, TOK], BF16)
    rsb = P.sbl("rsb", [128, TOK], BF16)
    logT = P.sbl("logT", [128, 16 * 24], F32)
    lst = P.sbl("lst", [24, 512], F32)
    esel = P.sbl("esel_b", [24, 24 * 128], BF16)
    Zb = P.sbl("Zb", [128, 256], F32)
    Tb = P.sbl("Tb", [128, 256], F32)
    Mb4 = P.sbl("Mb4", [128, 64], F32)
    ub = P.sbl("ub", [128, 64], F32)
    vb4 = P.sbl("vb4", [128, 64], F32)
    HR = P.sbl("HR", [24, TOK], BF16)
    Htm = P.sbl("Htm", [128, 16 * 24], F32)
    Hb = P.sbl("Hb", [128, 24 * 512], BF16)
    Xp = P.sbl("Xp", [128, 8 * 512], BF16)
    accb = [P.sbl("acc0", [128, 512], BF16), P.sbl("acc1", [128, 512], BF16)]
    tm1 = P.sbl("tm1", [128, 512], BF16)
    tm2 = P.sbl("tm2", [128, 512], BF16)

    Z4 = Zb[:].rearrange("p (s i j) -> p s i j", s=16, i=4)
    T4 = Tb[:].rearrange("p (s i j) -> p s i j", s=16, i=4)
    T4j = Tb[:].rearrange("p (s i j) -> p s j i", s=16, i=4)
    u3 = ub[:].rearrange("p (s i) -> p s i", s=16)
    v3 = vb4[:].rearrange("p (s j) -> p s j", s=16)
    M3 = Mb4[:, 0:64].rearrange("p (s i) -> p s i", s=16)

    with P.block_guard() as blk, ExitStack() as ps:
        psum_q = P.st_psum(ps, "ps_rms", [1, TOK], F32)
        p24 = P.st_psum(ps, "ps_l24", [24, 512], F32)
        pb = P.st_psum(ps, "ps_f1b", [128, 512], F32)
        ptp = P.st_psum(ps, "ps_f1t", [128, 128], F32)

        @blk.gpsimd
        def _(g):
            P.dma(g, "qg", wallb[:],
                  wall_d.rearrange("ip ic w -> ip (ic w)"))
            P.dma(g, "qg", gates[0:1, 0:1], gate_d)
            P.dma(g, "qg", esel[:], esel_d)
        wld = P.cnt["qg"]

        sq = {}
        mm_done = {}
        for c in range(NCH):
            @blk.vector
            def _(v, c=c):
                if c >= 1:
                    v.wait_ge(P.sems["pe"], mm_done[c - 1])
                P.bump(v.tensor_tensor(rsq[:], slot[:, c * TOK:(c + 1) * TOK],
                                       slot[:, c * TOK:(c + 1) * TOK],
                                       ALU.mult), "dve")
                sq[c] = P.cnt["dve"]

            @blk.tensor
            def _(t, c=c):
                t.wait_ge(P.sems["dve"], sq[c])
                ins = None
                for tt in range(4):
                    ins = t.matmul(psum_q[0:1, tt * 512:(tt + 1) * 512],
                                   onescb[:], rsq[:, tt * 512:(tt + 1) * 512],
                                   start=(c == 0), stop=(c == NCH - 1))
                P.bump(ins, "pe")
                mm_done[c] = P.cnt["pe"]
        rms_mm = P.cnt["pe"]

        @blk.vector
        def _(v):
            v.wait_ge(P.sems["pe"], rms_mm)
            P.bump(v.tensor_scalar(srow[:], psum_q[:], 1.0 / D,
                                   float(RMS_EPS), ALU.mult, ALU.add), "dve")
        ms_done = P.cnt["dve"]

        @blk.scalar
        def _(sc):
            sc.wait_ge(P.sems["dve"], ms_done)
            sc.activation(srow[:], srow[:], AF.Ln, bias=0.0, scale=1.0)
            P.bump(sc.activation(srow[:], srow[:], AF.Exp, bias=0.0,
                                 scale=-0.5), "act")
        rstd_done = P.cnt["act"]

        bcd = {}
        bcm = {}
        for tt in range(4):
            @blk.tensor
            def _(t, tt=tt):
                t.wait_ge(P.sems["act"], rstd_done)
                if tt >= 1:
                    t.wait_ge(P.sems["dve"], bcd[tt - 1])
                P.bump(t.matmul(pb[:], ones1b[:],
                                srow[0:1, tt * 512:(tt + 1) * 512],
                                start=True, stop=True), "pe")
                bcm[tt] = P.cnt["pe"]

            @blk.vector
            def _(v, tt=tt):
                v.wait_ge(P.sems["pe"], bcm[tt])
                P.bump(v.tensor_copy(rsb[:, tt * 512:(tt + 1) * 512], pb[:]),
                       "dve")
                bcd[tt] = P.cnt["dve"]
        rsb_done = P.cnt["dve"]

        lg_mm = {}
        lg_ev = {}
        tp_mm = {}
        tp_ev = {}
        for tt in range(4):
            @blk.tensor
            def _(t, tt=tt):
                t.wait_ge(P.sems["qg"], wld)
                if tt == 0:
                    t.wait_ge(P.sems["dve"], rsb_done)
                else:
                    t.wait_ge(P.sems["dve"], lg_ev[tt - 1])
                ins = None
                for ic in range(NCH):
                    ins = t.matmul(p24[:], wallb[:, ic * 24:(ic + 1) * 24],
                                   slot[:, ic * TOK + tt * 512:
                                        ic * TOK + (tt + 1) * 512],
                                   start=(ic == 0), stop=(ic == NCH - 1))
                P.bump(ins, "pe")
                lg_mm[tt] = P.cnt["pe"]

            @blk.vector
            def _(v, tt=tt):
                v.wait_ge(P.sems["pe"], lg_mm[tt])
                if tt >= 1:
                    v.wait_ge(P.sems["dve"], tp_ev[(tt - 1, 3)])
                v.tensor_tensor(lst[:], p24[:],
                                rsb[0:24, tt * 512:(tt + 1) * 512], ALU.mult)
                P.bump(v.tensor_scalar(lst[:], lst[:],
                                       b24t[:], None, ALU.add), "dve")
                lg_ev[tt] = P.cnt["dve"]

            for sb4 in range(4):
                @blk.tensor
                def _(t, tt=tt, sb4=sb4):
                    t.wait_ge(P.sems["dve"], lg_ev[tt])
                    prev = tp_ev.get((tt, sb4 - 1), tp_ev.get((tt - 1, 3)))
                    if prev:
                        t.wait_ge(P.sems["dve"], prev)
                    P.bump(t.transpose(ptp[:, 0:24],
                                       lst[:, sb4 * 128:(sb4 + 1) * 128],
                                       identf[0:24, 0:24]), "pe")
                    tp_mm[(tt, sb4)] = P.cnt["pe"]

                @blk.vector
                def _(v, tt=tt, sb4=sb4):
                    v.wait_ge(P.sems["pe"], tp_mm[(tt, sb4)])
                    P.bump(v.tensor_copy(
                        logT[:].rearrange("p (s w) -> p s w", s=16)
                        [:, tt * 4 + sb4, :], ptp[:, 0:24]), "dve")
                    tp_ev[(tt, sb4)] = P.cnt["dve"]
        log_done = P.cnt["dve"]

        @blk.vector
        def _(v):
            v.wait_ge(P.sems["qg"], wld)
            P.bump(v.memset(gates[0:1, 1:8], 0.0), "dve")
        gz_done = P.cnt["dve"]

        @blk.scalar
        def _(sc):
            sc.wait_ge(P.sems["dve"], gz_done)
            P.bump(sc.activation(gates[:], gates[:], AF.Exp, bias=0.0,
                                 scale=-1.0), "act")
        eg_done = P.cnt["act"]

        @blk.vector
        def _(v):
            v.wait_ge(P.sems["act"], eg_done)
            P.bump(v.tensor_scalar(gates[:], gates[:], 1.0, None, ALU.add),
                   "dve")
        sig1_done = P.cnt["dve"]

        @blk.scalar
        def _(sc):
            sc.wait_ge(P.sems["dve"], sig1_done)
            sc.activation(gates[:], gates[:], AF.Ln, bias=0.0, scale=1.0)
            P.bump(sc.activation(gates[:], gates[:], AF.Exp, bias=0.0,
                                 scale=-1.0), "act")
        sig_done = P.cnt["act"]

        @blk.tensor
        def _(t):
            t.wait_ge(P.sems["act"], sig_done)
            t.wait_ge(P.sems["dve"], log_done)
            P.bump(t.matmul(ptp[:, 0:1], ones1f[:], gates[0:1, 0:1],
                            start=True, stop=True), "pe")
        g_bc = P.cnt["pe"]

        @blk.vector
        def _(v):
            v.wait_ge(P.sems["pe"], g_bc)
            v.tensor_copy(g128[:], ptp[:, 0:1])
            v.tensor_scalar(gm1[:], ptp[:, 0:1], -1.0, 1.0, ALU.mult, ALU.add)
            v.tensor_scalar(
                Zb[:].rearrange("p (s w) -> p s w", s=16),
                logT[:].rearrange("p (s w) -> p s w", s=16)[:, :, 0:16],
                TAU_I, None, ALU.mult)
            v.memset(ub[:], 0.0)
            P.bump(v.memset(vb4[:], 0.0), "dve")
        z_done = P.cnt["dve"]

        tsub_done = {}
        exp_done = {}
        s_done = {}
        ln_done = {}
        for it in range(50):
            @blk.vector
            def _(v, it=it):
                if it == 0:
                    v.wait_ge(P.sems["dve"], z_done)
                else:
                    v.wait_ge(P.sems["act"], ln_done[(it - 1, 1)])
                    v.tensor_tensor(vb4[:], vb4[:], Mb4[:, 0:64], ALU.add)
                    v.tensor_scalar(vb4[:], vb4[:], -1.0, None, ALU.mult)
                v.tensor_tensor(T4, Z4, v3.unsqueeze(2).broadcast_to(
                    (128, 16, 4, 4)), ALU.add)
                v.tensor_reduce(M3, T4, AX.X, ALU.max)
                P.bump(v.tensor_tensor(T4, T4, M3.unsqueeze(3).broadcast_to(
                    (128, 16, 4, 4)), ALU.subtract), "dve")
                tsub_done[(it, 0)] = P.cnt["dve"]

            @blk.scalar
            def _(sc, it=it):
                sc.wait_ge(P.sems["dve"], tsub_done[(it, 0)])
                P.bump(sc.activation(Tb[:], Tb[:], AF.Exp), "act")
                exp_done[(it, 0)] = P.cnt["act"]

            @blk.vector
            def _(v, it=it):
                v.wait_ge(P.sems["act"], exp_done[(it, 0)])
                P.bump(v.tensor_reduce(u3, T4, AX.X, ALU.add), "dve")
                s_done[(it, 0)] = P.cnt["dve"]

            @blk.scalar
            def _(sc, it=it):
                sc.wait_ge(P.sems["dve"], s_done[(it, 0)])
                P.bump(sc.activation(ub[:], ub[:], AF.Ln), "act")
                ln_done[(it, 0)] = P.cnt["act"]

            @blk.vector
            def _(v, it=it):
                v.wait_ge(P.sems["act"], ln_done[(it, 0)])
                v.tensor_tensor(ub[:], ub[:], Mb4[:, 0:64], ALU.add)
                v.tensor_scalar(ub[:], ub[:], -1.0, None, ALU.mult)
                v.tensor_tensor(T4, Z4, u3.unsqueeze(3).broadcast_to(
                    (128, 16, 4, 4)), ALU.add)
                v.tensor_reduce(M3, T4j, AX.X, ALU.max)
                P.bump(v.tensor_tensor(
                    T4j, T4j, M3.unsqueeze(3).broadcast_to((128, 16, 4, 4)),
                    ALU.subtract), "dve")
                tsub_done[(it, 1)] = P.cnt["dve"]

            @blk.scalar
            def _(sc, it=it):
                sc.wait_ge(P.sems["dve"], tsub_done[(it, 1)])
                P.bump(sc.activation(Tb[:], Tb[:], AF.Exp), "act")
                exp_done[(it, 1)] = P.cnt["act"]

            @blk.vector
            def _(v, it=it):
                v.wait_ge(P.sems["act"], exp_done[(it, 1)])
                P.bump(v.tensor_reduce(v3, T4j, AX.X, ALU.add), "dve")
                s_done[(it, 1)] = P.cnt["dve"]

            @blk.scalar
            def _(sc, it=it):
                sc.wait_ge(P.sems["dve"], s_done[(it, 1)])
                P.bump(sc.activation(vb4[:], vb4[:], AF.Ln), "act")
                ln_done[(it, 1)] = P.cnt["act"]

        @blk.vector
        def _(v):
            v.wait_ge(P.sems["act"], ln_done[(49, 1)])
            v.tensor_tensor(vb4[:], vb4[:], Mb4[:, 0:64], ALU.add)
            v.tensor_scalar(vb4[:], vb4[:], -1.0, None, ALU.mult)
            v.tensor_tensor(T4, Z4, u3.unsqueeze(3).broadcast_to(
                (128, 16, 4, 4)), ALU.add)
            P.bump(v.tensor_tensor(T4, T4, v3.unsqueeze(2).broadcast_to(
                (128, 16, 4, 4)), ALU.add), "dve")
        h_arg = P.cnt["dve"]

        @blk.scalar
        def _(sc):
            sc.wait_ge(P.sems["dve"], h_arg)
            P.bump(sc.activation(Tb[:], Tb[:], AF.Exp), "act")
        h_exp = P.cnt["act"]

        @blk.vector
        def _(v):
            v.wait_ge(P.sems["act"], h_exp)
            v.tensor_scalar(Tb[:], Tb[:], g128[:], None, ALU.mult)
            v.tensor_scalar(
                Tb[:].rearrange("p (s w) -> p s w", s=16)[:, :, 0:16:5],
                Tb[:].rearrange("p (s w) -> p s w", s=16)[:, :, 0:16:5],
                gm1[:], None, ALU.add)
            v.tensor_copy(
                Htm[:].rearrange("p (s w) -> p s w", s=16, w=24)[:, :, 0:16],
                Tb[:].rearrange("p (s w) -> p s w", s=16, w=16))
            for off, dst in ((16, 16), (20, 20)):
                lg = logT[:].rearrange("p (s w) -> p s w", s=16, w=24)[
                    :, :, off:off + 4]
                h4 = Htm[:].rearrange("p (s w) -> p s w", s=16, w=24)[
                    :, :, dst:dst + 4]
                m2 = Mb4[:, 0:16].rearrange("p (s o) -> p s o", s=16)
                v.tensor_reduce(m2, lg, AX.X, ALU.max)
                v.tensor_tensor(h4, lg, m2.broadcast_to((128, 16, 4)),
                                ALU.subtract)
            P.bump(v.tensor_copy(Htm[0:1, 0:1], Htm[0:1, 0:1]), "dve")
        hp_sub = P.cnt["dve"]

        @blk.scalar
        def _(sc):
            sc.wait_ge(P.sems["dve"], hp_sub)
            P.bump(sc.activation(
                Htm[:].rearrange("p (s w) -> p s w", s=16, w=24)[:, :, 16:24],
                Htm[:].rearrange("p (s w) -> p s w", s=16, w=24)[:, :, 16:24],
                AF.Exp), "act")
        hp_exp = P.cnt["act"]

        @blk.vector
        def _(v):
            v.wait_ge(P.sems["act"], hp_exp)
            h8 = Htm[:].rearrange("p (s w) -> p s w", s=16, w=24)[
                :, :, 16:24].rearrange("p s (k o) -> p s k o", k=2)
            m8 = Mb4[:, 0:32].rearrange("p (s k) -> p s k", s=16).unsqueeze(3)
            P.bump(v.tensor_reduce(m8, h8, AX.X, ALU.add), "dve")
        hs_done = P.cnt["dve"]

        @blk.scalar
        def _(sc):
            sc.wait_ge(P.sems["dve"], hs_done)
            sc.activation(Mb4[:, 0:32], Mb4[:, 0:32], AF.Ln, bias=0.0,
                          scale=1.0)
            P.bump(sc.activation(Mb4[:, 0:32], Mb4[:, 0:32], AF.Exp, bias=0.0,
                                 scale=-1.0), "act")
        hsr_done = P.cnt["act"]

        @blk.vector
        def _(v):
            v.wait_ge(P.sems["act"], hsr_done)
            h8 = Htm[:].rearrange("p (s w) -> p s w", s=16, w=24)[
                :, :, 16:24].rearrange("p s (k o) -> p s k o", k=2)
            m8 = Mb4[:, 0:32].rearrange("p (s k) -> p s k", s=16).unsqueeze(3)
            v.tensor_tensor(h8, h8, m8.broadcast_to((128, 16, 2, 4)),
                            ALU.mult)
            P.bump(v.tensor_copy(Htm[0:1, 0:1], Htm[0:1, 0:1]), "dve")
        htm_done = P.cnt["dve"]

        tphm = {}
        tph = {}
        for s16 in range(16):
            @blk.tensor
            def _(t, s16=s16):
                t.wait_ge(P.sems["dve"], htm_done if s16 == 0
                          else tph[s16 - 1])
                P.bump(t.transpose(ptp[0:24, 0:128],
                                   Htm[:, s16 * 24:(s16 + 1) * 24],
                                   identf[:]), "pe")
                tphm[s16] = P.cnt["pe"]

            @blk.vector
            def _(v, s16=s16):
                v.wait_ge(P.sems["pe"], tphm[s16])
                P.bump(v.tensor_copy(HR[:, s16 * 128:(s16 + 1) * 128],
                                     ptp[0:24, 0:128]), "dve")
                tph[s16] = P.cnt["dve"]
        hr_done = P.cnt["dve"]

        if "dbg:ffn1" in P.externals:
            d_log = P.dram("d_log", [128, 384], F32, io="out")
            d_htm = P.dram("d_htm", [128, 384], F32, io="out")
            d_u = P.dram("d_u", [128, 64], F32, io="out")
            d_v = P.dram("d_v", [128, 64], F32, io="out")
            d_hr = P.dram("d_hr", [24, TOK], BF16, io="out")
            d_g = P.dram("d_g", [128, 1], F32, io="out")
            d_gm = P.dram("d_gm", [128, 1], F32, io="out")
            d_tb = P.dram("d_tb", [128, 256], F32, io="out")

            @blk.sync
            def _(s):
                s.wait_ge(P.sems["dve"], hr_done)
                P.dma(s, "qs", d_log, logT[:])
                P.dma(s, "qs", d_htm, Htm[:])
                P.dma(s, "qs", d_u, ub[:])
                P.dma(s, "qs", d_v, vb4[:])
                P.dma(s, "qs", d_hr, HR[0:24, :])
                P.dma(s, "qs", d_g, g128[:])
                P.dma(s, "qs", d_gm, gm1[:])
                P.dma(s, "qs", d_tb, Tb[:])
                s.wait_ge(P.sems["qs"], P.cnt["qs"])

    with P.block_guard() as blk, ExitStack() as ps:
        pbm = P.st_psum(ps, "ps_f1bm", [128, 512], F32)
        py = [P.st_psum(ps, f"ps_f1y{j}", [128, 512], F32) for j in range(2)]
        y_done = {}
        mix_done = {}
        st_done = {}
        bc_done = {}
        bc_mm = {}
        acc_free = {0: 0, 1: 0}
        wbr_ld = {}
        for tt in range(4):
            for r in range(24):
                @blk.tensor
                def _(t, tt=tt, r=r):
                    if tt == 0 and r == 0:
                        t.wait_ge(P.sems["dve"], hr_done)
                    else:
                        t.wait_ge(P.sems["dve"], bc_done[(tt, r - 1)]
                                  if r else bc_done[(tt - 1, 23)])
                    if r == 0 and tt >= 1:
                        t.wait_ge(P.sems["dve"],
                                  mix_done[(tt - 1, 7, 3)])
                    P.bump(t.matmul(pbm[:],
                                    esel[:, r * 128:(r + 1) * 128],
                                    HR[0:24, tt * 512:(tt + 1) * 512],
                                    start=True, stop=True), "pe")
                    bc_mm[(tt, r)] = P.cnt["pe"]

                @blk.vector
                def _(v, tt=tt, r=r):
                    v.wait_ge(P.sems["pe"], bc_mm[(tt, r)])
                    P.bump(v.tensor_copy(Hb[:, r * 512:(r + 1) * 512],
                                         pbm[:]), "dve")
                    bc_done[(tt, r)] = P.cnt["dve"]
            hb_done = P.cnt["dve"]

            @blk.vector
            def _(v, tt=tt, hb_done=hb_done):
                v.wait_ge(P.sems["dve"], hb_done)
                if tt >= 1:
                    v.wait_ge(P.sems["pe"], y_done[(tt - 1, 7)])
                for cc in range(8):
                    xp = Xp[:, cc * 512:(cc + 1) * 512]
                    v.tensor_tensor(
                        xp, slot[:, cc * TOK + tt * 512:
                                 cc * TOK + (tt + 1) * 512],
                        Hb[:, 16 * 512:17 * 512], ALU.mult)
                    for n in range(1, 4):
                        v.tensor_tensor(
                            tm1[:], slot[:, (n * 8 + cc) * TOK + tt * 512:
                                         (n * 8 + cc) * TOK + (tt + 1) * 512],
                            Hb[:, (16 + n) * 512:(17 + n) * 512], ALU.mult)
                        v.tensor_tensor(xp, xp, tm1[:], ALU.add)
                P.bump(v.tensor_copy(Xp[0:1, 0:1], Xp[0:1, 0:1]), "dve")
            xp_done = P.cnt["dve"]

            for ocp in range(8):
                @blk.gpsimd
                def _(g, tt=tt, ocp=ocp):
                    pk = (tt, ocp - 2) if ocp >= 2 else \
                        ((tt - 1, ocp + 6) if tt >= 1 else None)
                    if pk is not None and pk in y_done:
                        g.wait_ge(P.sems["pe"], y_done[pk])
                    P.dma(g, "qg", wbrw[ocp % 2][:],
                          wbr_d[ocp].rearrange("ip ic op -> ip (ic op)"))
                    wbr_ld[(tt, ocp)] = P.cnt["qg"]

                @blk.tensor
                def _(t, tt=tt, ocp=ocp, xp_done=xp_done):
                    t.wait_ge(P.sems["dve"], xp_done)
                    t.wait_ge(P.sems["qg"], wbr_ld[(tt, ocp)])
                    pk = ((tt, ocp - 2, 3) if ocp >= 2 else
                          ((tt - 1, ocp + 6, 3) if tt >= 1 else None))
                    if pk is not None and pk in mix_done:
                        t.wait_ge(P.sems["dve"], mix_done[pk])
                    ins = None
                    for icp in range(8):
                        ins = t.matmul(
                            py[ocp % 2][:],
                            wbrw[ocp % 2][:, icp * 128:(icp + 1) * 128],
                            Xp[:, icp * 512:(icp + 1) * 512],
                            start=(icp == 0), stop=(icp == 7))
                    P.bump(ins, "pe")
                    y_done[(tt, ocp)] = P.cnt["pe"]

                for i in range(4):
                    @blk.vector
                    def _(v, tt=tt, ocp=ocp, i=i):
                        if i == 0:
                            v.wait_ge(P.sems["pe"], y_done[(tt, ocp)])
                        if acc_free[i % 2]:
                            v.wait_ge(P.sems["qs"], acc_free[i % 2])
                        ch = i * 8 + ocp
                        a = accb[i % 2]
                        v.tensor_tensor(
                            a[:], slot[:, ocp * TOK + tt * 512:
                                       ocp * TOK + (tt + 1) * 512],
                            Hb[:, (i * 4) * 512:(i * 4 + 1) * 512], ALU.mult)
                        for j in range(1, 4):
                            v.tensor_tensor(
                                tm1[:],
                                slot[:, (j * 8 + ocp) * TOK + tt * 512:
                                     (j * 8 + ocp) * TOK + (tt + 1) * 512],
                                Hb[:, (i * 4 + j) * 512:(i * 4 + j + 1) * 512],
                                ALU.mult)
                            v.tensor_tensor(a[:], a[:], tm1[:], ALU.add)
                        v.tensor_tensor(tm2[:], py[ocp % 2][:],
                                        Hb[:, (20 + i) * 512:(21 + i) * 512],
                                        ALU.mult)
                        v.tensor_tensor(a[:], a[:], tm2[:], ALU.add)
                        P.bump(v.tensor_tensor(
                            a[:], a[:], slot[:, ch * TOK + tt * 512:
                                             ch * TOK + (tt + 1) * 512],
                            ALU.add), "dve")
                        mix_done[(tt, ocp, i)] = P.cnt["dve"]

                    @blk.sync
                    def _(s, tt=tt, ocp=ocp, i=i):
                        s.wait_ge(P.sems["dve"], mix_done[(tt, ocp, i)])
                        ch = i * 8 + ocp
                        P.dma(s, "qs",
                              rbuf[ch * 128:(ch + 1) * 128,
                                   tt * 512:(tt + 1) * 512], accb[i % 2][:])
                        st_done[(tt, ocp, i)] = P.cnt["qs"]
                        acc_free[i % 2] = P.cnt["qs"]

        @blk.sync
        def _(s):
            s.wait_ge(P.sems["qs"], P.cnt["qs"])



PHASES_ALL = ["init", "qkv0", "gather0", "attn0", "oproj0", "ln1",
              "ffn0", "ln2", "qkv1", "gather1", "attn1", "oproj1", "ln3",
              "ffn1", "ln4"]


def build_program(phases, externals=()):
    P = Prog(externals=externals)
    for pi, ph in enumerate(phases):
        P._phase_idx = pi
        P._local_st = ExitStack()
        P._local_names = []
        if ph == "init":
            emit_init(P)
        elif ph == "initc":
            emit_init(P, with_x=False)
        elif ph == "qkv0":
            emit_proj(P, 0, "qkv")
        elif ph == "qkv1":
            emit_proj(P, 1, "qkv")
        elif ph == "gather0":
            emit_gather(P, 0)
        elif ph == "gather1":
            emit_gather(P, 1)
        elif ph == "attn0":
            emit_attn(P, 0)
        elif ph == "attn1":
            emit_attn(P, 1)
        elif ph == "oproj0":
            emit_proj(P, 0, "oproj")
        elif ph == "oproj1":
            emit_proj(P, 1, "oproj")
        elif ph in ("ln1", "ln2", "ln3"):
            emit_ln(P, final=False)
        elif ph == "ln4":
            emit_ln(P, final=True)
        elif ph == "ffn0":
            emit_ffn0(P)
        elif ph == "ffn1":
            emit_ffn1(P)
        elif ph == "loadslot":
            emit_loadslot(P)
        elif ph == "saveslot":
            emit_saveslot(P)
        elif ph == "saveslot2":
            emit_saveslot(P, "slotdram2")
        else:
            raise ValueError(ph)
        for n in P._local_names:
            del P.sb_t[n]
        P._local_st.close()
    return P


def emit_loadslot(P):
    slot = P.sb("slot", [128, NCH * TOK], BF16)
    sdr = P.dram("slotdram", [D, TOK], BF16, io="x")
    with P.block_guard() as blk:
        @blk.sync
        def _(s):
            for c in range(NCH):
                P.dma(s, "qs", slot[:, c * TOK:(c + 1) * TOK],
                      sdr[c * 128:(c + 1) * 128, :])
            s.wait_ge(P.sems["qs"], P.cnt["qs"])


def emit_saveslot(P, name="slotdram"):
    slot = P.sb("slot", [128, NCH * TOK], BF16)
    sdr = P.dram(name, [D, TOK], BF16, io="x")
    with P.block_guard() as blk:
        @blk.sync
        def _(s):
            for c in range(NCH):
                P.dma(s, "qs", sdr[c * 128:(c + 1) * 128, :],
                      slot[:, c * TOK:(c + 1) * TOK])
            s.wait_ge(P.sems["qs"], P.cnt["qs"])



def kernel(**inputs):
    shared = prep_shared(inputs)
    cosT, sinT = _rope_tables()
    cores = [prep_core(inputs, c, cosT, sinT) for c in range(8)]

    P = build_program(PHASES_ALL)
    in_maps = []
    for c in range(8):
        m = dict(shared)
        m["identb_d"] = shared["identb"]
        m["identf_d"] = shared["identf"]
        m.update(cores[c])
        in_maps.append({k: v for k, v in m.items() if k in P.dram_t})
    res = run_bass_kernel_spmd(P.nc, in_maps, core_ids=list(range(8)))
    P.close()
    outs = [np.asarray(r["y"], np.float32) for r in res.results]
    return unprep_out(outs)


# version 08 (final)
# speedup vs baseline: 1.1414x; 1.1414x over previous
import numpy as np
import ml_dtypes
from contextlib import ExitStack

import concourse.bass as bass
import concourse.mybir as mybir
from concourse.bass_utils import run_bass_kernel_spmd

F32 = mybir.dt.float32
BF16 = mybir.dt.bfloat16
I32 = mybir.dt.int32
AF = mybir.ActivationFunctionType
ALU = mybir.AluOpType
AX = mybir.AxisListType
BF = ml_dtypes.bfloat16

B, T, D = 4, 4096, 4096
HD, NH = 128, 32
TOK = 2048
NCH = 32
SP = [2, 4, 6, 8]
OWN = [[0, 3, 4, 7], [1, 2, 5, 6]]
OWNER = [0, 1, 1, 0, 0, 1, 1, 0]
POS = [OWN[OWNER[g]].index(g) for g in range(8)]
ISCALE = float(1.0 / np.sqrt(128.0))
LN_EPS = 1e-5
RMS_EPS = 1.1920929e-07
TAU_I = 10.0



def _tile_w(w):
    i, o = w.shape
    return np.ascontiguousarray(
        np.asarray(w).reshape(i // 128, 128, o // 128, 128).transpose(2, 1, 0, 3))


def _tile_sc(sc):
    sc = np.asarray(sc)
    g, w = sc.shape
    return np.ascontiguousarray(
        sc.reshape(g, w // 128, 128).transpose(1, 0, 2).reshape(w // 128, g * 128))


def _rope_tables():
    inv = 1.0 / (10000.0 ** (np.arange(0, HD, 2, dtype=np.float64) / HD))
    fr = np.arange(T, dtype=np.float64)[:, None] * inv[None, :]
    emb = np.concatenate([fr, fr], axis=-1)
    cos = np.cos(emb).astype(np.float32)
    sin = np.sin(emb).astype(np.float32)
    sgn = np.ones((HD,), np.float32)
    sgn[:64] = -1.0
    return (np.ascontiguousarray(cos.T),
            np.ascontiguousarray(sin.T * sgn[:, None]))


def prep_shared(i):
    s = {}
    s["wq0"] = _tile_w(np.asarray(i["L0_attn_wq_w"]).T)
    s["wk0"] = _tile_w(np.asarray(i["L0_attn_wk_w"]).T)
    s["wv0"] = _tile_w(np.asarray(i["L0_attn_wv_w"]).T)
    s["wo0"] = _tile_w(np.asarray(i["L0_attn_wo_w"]).T)
    s["sg0"] = _tile_w(np.asarray(i["L0_ffn_sign"]).T)
    s["qb0"] = np.asarray(i["L0_attn_wq_b"]).reshape(32, 128)
    s["kb0"] = np.asarray(i["L0_attn_wk_b"]).reshape(1, 128)
    s["vb0"] = np.asarray(i["L0_attn_wv_b"]).reshape(1, 128)
    s["ob0"] = np.asarray(i["L0_attn_wo_b"]).reshape(32, 128)
    s["fs0"] = np.asarray(i["L0_ffn_scale"]).reshape(32, 128)
    s["fb0"] = np.asarray(i["L0_ffn_bias"]).reshape(32, 128)
    s["qwq"] = _tile_w(i["L1_attn_qw_q"])
    s["qwk"] = _tile_w(i["L1_attn_qw_k"])
    s["qwv"] = _tile_w(i["L1_attn_qw_v"])
    s["qwo"] = _tile_w(i["L1_attn_qw_o"])
    s["scq"] = _tile_sc(i["L1_attn_scales_q"])
    s["sck"] = _tile_sc(i["L1_attn_scales_k"])
    s["scv"] = _tile_sc(i["L1_attn_scales_v"])
    s["sco"] = _tile_sc(i["L1_attn_scales_o"])
    s["wall"] = np.ascontiguousarray(
        np.concatenate([np.asarray(i["L1_ffn_W_res"]),
                        np.asarray(i["L1_ffn_W_pre"]),
                        np.asarray(i["L1_ffn_W_post"])], axis=0)
        .T.reshape(32, 128, 24).transpose(1, 0, 2))
    s["b24"] = np.concatenate(
        [np.asarray(i["L1_ffn_H_res_base"]).ravel(),
         np.asarray(i["L1_ffn_H_pre_base"]),
         np.asarray(i["L1_ffn_H_post_base"])]).reshape(1, 24).astype(np.float32)
    s["gate"] = np.asarray(i["L1_ffn_gate"]).reshape(1, 1).astype(np.float32)
    s["wbr"] = _tile_w(i["L1_ffn_W_branch"])
    s["identb"] = np.eye(128, dtype=np.float32).astype(BF)
    s["identf"] = np.eye(128, dtype=np.float32)
    esel = np.zeros((24, 24, 128), np.float32)
    for r in range(24):
        esel[r, r, :] = 1.0
    s["esel"] = np.ascontiguousarray(esel.transpose(1, 0, 2)
                                     .reshape(24, 24 * 128)).astype(BF)
    return s


def prep_core(i, core, cosT, sinT):
    b, p = core // 2, core % 2
    own = OWN[p]
    d = {}
    xT = np.ascontiguousarray(np.asarray(i["L0_attn_x"][b]).T)
    cols = np.concatenate([np.arange(g * 512, (g + 1) * 512) for g in own])
    d["x"] = np.ascontiguousarray(xT[:, cols])
    d["cosT"] = np.ascontiguousarray(cosT[:, cols]).astype(BF)
    d["sinT"] = np.ascontiguousarray(sinT[:, cols]).astype(BF)
    colv = np.arange(1408)[None, :]
    rowv = np.arange(128)[:, None]
    m = np.zeros((2, 128, 1408), np.float32)
    for g in range(2):
        thr = 384 if (p == 0) == (g == 0) else 896
        m[g] = np.where(colv <= thr + rowv, 0.0, -30000.0)
    d["mext"] = m.astype(BF)
    return d


def unprep_out(per_core):
    out = np.empty((B, T, D), np.float32)
    for c in range(8):
        b, p = c // 2, c % 2
        y = per_core[c]
        for lb, g in enumerate(OWN[p]):
            out[b, g * 512:(g + 1) * 512, :] = y[:, lb * 512:(lb + 1) * 512].T
    return out



class Prog:

    def __init__(self, externals=()):
        self.nc = bass.Bass("TRN2", target_bir_lowering=False)
        self.st = ExitStack()
        self.externals = set(externals)
        self.cnt = {}
        self.sems = {}
        self.dram_t = {}
        self.sb_t = {}
        for name in ["qg", "qs", "pe", "act", "dve", "cc"]:
            self.sems[name] = self.st.enter_context(self.nc.semaphore(name))
            self.cnt[name] = 0

    def dram(self, name, shape, dtype, io=None):
        if name in self.dram_t:
            return self.dram_t[name]
        kind = None
        if io == "in":
            kind = "ExternalInput"
        elif io == "out":
            kind = "ExternalOutput"
        elif io == "x":
            if ("in:" + name) in self.externals:
                kind = "ExternalInput"
            elif ("out:" + name) in self.externals:
                kind = "ExternalOutput"
        t = (self.nc.dram_tensor(name, shape, dtype, kind=kind).ap() if kind
             else self.nc.dram_tensor(name, shape, dtype).ap())
        self.dram_t[name] = t
        return t

    def sb(self, name, shape, dtype):
        if name not in self.sb_t:
            self.sb_t[name] = self.st.enter_context(
                self.nc.sbuf_tensor(name, shape, dtype))
        return self.sb_t[name]

    def sbl(self, name, shape, dtype):
        if name in self.sb_t:
            return self.sb_t[name]
        t = self._local_st.enter_context(
            self.nc.sbuf_tensor(f"{name}_p{self._phase_idx}", shape, dtype))
        self.sb_t[name] = t
        self._local_names.append(name)
        return t

    def dma(self, q, qname, out, in_):
        q.dma_start(out=out, in_=in_).then_inc(self.sems[qname], 16)
        self.cnt[qname] += 16
        return self.cnt[qname]

    def bump(self, instr, ename):
        instr.then_inc(self.sems[ename], 1)
        self.cnt[ename] += 1
        return self.cnt[ename]

    def close(self):
        self.st.close()


def section(blk, engine_name):
    return getattr(blk, engine_name)



def emit_init(P, with_x=True):
    nc = P.nc
    slot = P.sb("slot", [128, NCH * TOK], BF16)
    x = P.dram("x", [D, TOK], F32, io="in")
    xres = P.dram("xres", [D, TOK], BF16, io="x")
    mext_d = P.dram("mext", [2, 128, 1408], BF16, io="in")
    cosT_d = P.dram("cosT", [128, TOK], BF16, io="in")
    sinT_d = P.dram("sinT", [128, TOK], BF16, io="in")
    identb_d = P.dram("identb_d", [128, 128], BF16, io="in")
    identf_d = P.dram("identf_d", [128, 128], F32, io="in")
    mb = P.sb("mb", [128, 2 * 1408], BF16)
    identb = P.sb("identb", [128, 128], BF16)
    identf = P.sb("identf", [128, 128], F32)
    ones1b = P.sb("ones1b", [1, 128], BF16)
    ones1f = P.sb("ones1f", [1, 128], F32)
    onescb = P.sb("onescb", [128, 1], BF16)
    vecs = [("qb0", 32), ("ob0", 32), ("fs0", 32), ("fb0", 32),
            ("kb0", 1), ("vb0", 1)]
    for n, w in vecs:
        P.dram(n, [w, 128], F32, io="in")
        P.sb(n + "_t", [128, w], F32)
    P.dram("b24", [1, 24], F32, io="in")
    P.sb("b24_t", [24, 1], F32)
    for n, w in vecs:
        P.sbl(n + "_r", [w, 128], F32)
    P.sbl("b24_r", [1, 24], F32)

    with P.block_guard() as blk, ExitStack() as ps:
        pt = P.st_psum(ps, "pi_tr", [128, 128], F32)

        @blk.gpsimd
        def _(g):
            if with_x:
                for c in range(NCH):
                    P.dma(g, "qg", slot[:, c * TOK:(c + 1) * TOK],
                          x[c * 128:(c + 1) * 128, :])
            P.dma(g, "qg", mb[:].rearrange("p (gr w) -> p gr w", gr=2),
                  mext_d.rearrange("gr p w -> p gr w"))
            P.dma(g, "qg", identb[:], identb_d)
            P.dma(g, "qg", identf[:], identf_d)
            for n, w in vecs:
                P.dma(g, "qg", P.sb_t[n + "_r"][:], P.dram_t[n])
            P.dma(g, "qg", P.sb_t["b24_r"][:], P.dram_t["b24"])
        qg_all = P.cnt["qg"]

        @blk.vector
        def _(v):
            v.memset(ones1b[:], 1.0)
            v.memset(ones1f[:], 1.0)
            P.bump(v.memset(onescb[:], 1.0), "dve")

        for n, w in vecs + [("b24", 24)]:
            @blk.tensor
            def _(t, n=n, w=w):
                t.wait_ge(P.sems["qg"], qg_all)
                t.wait_ge(P.sems["dve"], P.cnt["dve"])
                if n == "b24":
                    P.bump(t.transpose(pt[0:24, 0:1], P.sb_t[n + "_r"][:],
                                       identf[0:1, 0:1]), "pe")
                else:
                    P.bump(t.transpose(pt[:, 0:w], P.sb_t[n + "_r"][:],
                                       identf[0:w, 0:w]), "pe")

            @blk.vector
            def _(v, n=n, w=w):
                v.wait_ge(P.sems["pe"], P.cnt["pe"])
                if n == "b24":
                    P.bump(v.tensor_copy(P.sb_t[n + "_t"][:], pt[0:24, 0:1]),
                           "dve")
                elif n == "qb0":
                    P.bump(v.tensor_scalar(P.sb_t[n + "_t"][:], pt[:, 0:w],
                                           ISCALE, None, ALU.mult), "dve")
                else:
                    P.bump(v.tensor_copy(P.sb_t[n + "_t"][:], pt[:, 0:w]),
                           "dve")

        if with_x:
            @blk.sync
            def _(s):
                s.wait_ge(P.sems["qg"], 16 * NCH)
                for c in range(NCH):
                    P.dma(s, "qs", xres[c * 128:(c + 1) * 128, :],
                          slot[:, c * TOK:(c + 1) * TOK])
                s.wait_ge(P.sems["qs"], P.cnt["qs"])


def _block_guard(self):
    return self.nc.Block()


def _preload_tables(self, sc):
    from concourse.hw_specs import get_activation_tables
    import concourse.mybir as _mybir
    tabs = get_activation_tables(self.nc.m.arch)
    idx = list(tabs).index("natural_log_exp_and_others")
    sc.add_instruction(_mybir.InstLoadActFuncSet(
        name=self.nc.get_next_instruction_name(),
        act_func_set_id=idx, ins=[], outs=[]))


def _st_psum(self, stack, name, shape, dtype=F32):
    return stack.enter_context(self.nc.psum_tensor(
        f"{name}_p{getattr(self, '_phase_idx', 0)}", shape, dtype))


Prog.block_guard = _block_guard
Prog.st_psum = _st_psum
Prog.preload_tables = _preload_tables



def emit_proj(P, layer, mode):
    nc = P.nc
    dq = (layer == 1)
    rope = (layer == 1)
    kvh = 1 if layer == 0 else 8
    slot = P.sb_t["slot"]
    wbuf = [P.sbl("wbuf0", [128, 4096], BF16), P.sbl("wbuf1", [128, 4096], BF16)]
    qst2 = [P.sbl("qst0", [128, TOK], BF16), P.sbl("qst1", [128, TOK], BF16)]
    if mode == "oproj":
        rst = P.sbl("rst", [128, TOK], BF16)
        resk = P.sbl("resk", [128, TOK], BF16)
    if dq:
        wdq = P.sbl("wdq", [128, 4096], BF16)
        scrow = [P.sbl("scrow0", [1, 4096], BF16), P.sbl("scrow1", [1, 4096], BF16)]
    if rope and mode == "qkv":
        qtmp = P.sbl("qtmp", [128, TOK], BF16)
        cosb = P.sbl("cosb", [128, TOK], BF16)
        sinb = P.sbl("sinb", [128, TOK], BF16)
    if mode == "qkv":
        vtm = P.sbl("vtm", [128, TOK], BF16)
    ones1b = P.sb_t["ones1b"]
    identb = P.sb_t["identb"]

    if mode == "qkv":
        qd = P.dram("qd", [D, TOK], BF16, io="x")
        ko = P.dram(f"k_own{layer}", [kvh * 128, TOK], BF16, io="x")
        vo = P.dram(f"v_own{layer}", [TOK, kvh * 128], BF16, io="x")
        if layer == 0:
            wq_d = P.dram("wq0", [32, 128, 32, 128], F32, io="in")
            wk_d = P.dram("wk0", [1, 128, 32, 128], F32, io="in")
            wv_d = P.dram("wv0", [1, 128, 32, 128], F32, io="in")
            tasks = ([("q", oc, wq_d, None, qd, oc * 128) for oc in range(32)]
                     + [("k", 0, wk_d, None, ko, 0),
                        ("v", 0, wv_d, None, vo, 0)])
        else:
            wq_d = P.dram("qwq", [32, 128, 32, 128], I32, io="in")
            wk_d = P.dram("qwk", [8, 128, 32, 128], I32, io="in")
            wv_d = P.dram("qwv", [8, 128, 32, 128], I32, io="in")
            scq_d = P.dram("scq", [32, 4096], F32, io="in")
            sck_d = P.dram("sck", [8, 4096], F32, io="in")
            scv_d = P.dram("scv", [8, 4096], F32, io="in")
            tasks = ([("q", oc, wq_d, scq_d, qd, oc * 128) for oc in range(32)]
                     + [("k", oc, wk_d, sck_d, ko, oc * 128) for oc in range(8)]
                     + [("v", oc, wv_d, scv_d, vo, oc * 128) for oc in range(8)])
    else:
        rbuf = P.dram("rbuf", [D, TOK], BF16, io="x")
        xres = P.dram("xres", [D, TOK], BF16, io="x")
        if layer == 0:
            wo_d = P.dram("wo0", [32, 128, 32, 128], F32, io="in")
            tasks = [("o", oc, wo_d, None, rbuf, oc * 128) for oc in range(32)]
        else:
            wo_d = P.dram("qwo", [32, 128, 32, 128], I32, io="in")
            sco_d = P.dram("sco", [32, 4096], F32, io="in")
            tasks = [("o", oc, wo_d, sco_d, rbuf, oc * 128) for oc in range(32)]

    with P.block_guard() as blk, ExitStack() as ps:
        pp = P.st_psum(ps, f"pp_{mode}{layer}", [128, TOK], F32)
        if dq:
            pbc = P.st_psum(ps, f"pbc_{mode}{layer}", [128, 512], F32)
        if mode == "qkv":
            ptr = P.st_psum(ps, f"ptr_{mode}{layer}", [128, TOK], BF16)

        if rope and mode == "qkv":
            @blk.gpsimd
            def _(g):
                P.dma(g, "qg", cosb[:], P.dram_t["cosT"])
                P.dma(g, "qg", sinb[:], P.dram_t["sinT"])

        mm_done = {}
        vtm_done = {}
        tr_done = {}
        ev_done = {}
        st_done = {}
        ep_done = {}
        ld_done = {}

        for i, (kind, oc, wd, scd, dst, row) in enumerate(tasks):
            @blk.gpsimd
            def _(g, i=i):
                def load(j):
                    kj, ocj, wdj, scdj, _d, _r = tasks[j]
                    if j >= 2 and (j - 2) in mm_done:
                        g.wait_ge(P.sems["pe"], mm_done[j - 2])
                    P.dma(g, "qg", wbuf[j % 2][:],
                          wdj[ocj].rearrange("ip ic op -> ip (ic op)"))
                    if dq:
                        P.dma(g, "qg", scrow[j % 2][:],
                              scdj[ocj].unsqueeze(0))
                    ld_done[j] = P.cnt["qg"]
                if i == 0:
                    load(0)
                    if len(tasks) > 1:
                        load(1)
                elif i + 1 < len(tasks):
                    load(i + 1)

            if mode == "oproj":
                @blk.sync
                def _(s, i=i, row=row):
                    if i >= 1:
                        s.wait_ge(P.sems["dve"], ep_done[i - 1])
                    P.dma(s, "qs", resk[:], xres[row:row + 128, :])
                    ld_done[("r", i)] = P.cnt["qs"]

            if dq:
                dve0 = P.cnt["dve"]
                pe0 = P.cnt["pe"]

                @blk.tensor
                def _(t, i=i, dve0=dve0):
                    t.wait_ge(P.sems["qg"], ld_done[i])
                    if i >= 1:
                        t.wait_ge(P.sems["act"], ev_done[i - 1])
                    for s in range(8):
                        if s >= 1:
                            t.wait_ge(P.sems["dve"], dve0 + s)
                        P.bump(t.matmul(pbc[:], ones1b[:],
                                        scrow[i % 2][0:1, s * 512:(s + 1) * 512],
                                        start=True, stop=True), "pe")

                @blk.vector
                def _(v, i=i, pe0=pe0):
                    if i >= 1:
                        v.wait_ge(P.sems["pe"], mm_done[i - 1])
                    for s in range(8):
                        v.wait_ge(P.sems["pe"], pe0 + s + 1)
                        P.bump(v.scalar_tensor_tensor(
                            wdq[:, s * 512:(s + 1) * 512],
                            wbuf[i % 2][:, s * 512:(s + 1) * 512],
                            -8.0, pbc[:], ALU.add, ALU.mult), "dve")
                dq_dve_done = P.cnt["dve"]

            @blk.tensor
            def _(t, i=i):
                src = wdq if dq else wbuf[i % 2]
                if not dq:
                    t.wait_ge(P.sems["qg"], ld_done[i])
                else:
                    t.wait_ge(P.sems["dve"], dq_dve_done)
                if i >= 1:
                    t.wait_ge(P.sems["act"], ev_done[i - 1])
                for ic in range(NCH):
                    for tt in range(4):
                        ins = t.matmul(
                            pp[:, tt * 512:(tt + 1) * 512],
                            src[:, ic * 128:(ic + 1) * 128],
                            slot[:, ic * TOK + tt * 512:ic * TOK + (tt + 1) * 512],
                            start=(ic == 0), stop=(ic == NCH - 1))
                        if ic == NCH - 1 and tt == 3:
                            P.bump(ins, "pe")
                mm_done[i] = P.cnt["pe"]

            @blk.scalar
            def _(sc, i=i, kind=kind, oc=oc):
                qst = qst2[i % 2]
                sc.wait_ge(P.sems["pe"], mm_done[i])
                if i >= 2:
                    sc.wait_ge(P.sems["qs"], st_done[i - 2])
                if mode == "qkv":
                    if kind == "q":
                        bias = (P.sb_t["qb0_t"][:, oc:oc + 1] if layer == 0
                                else 0.0)
                        P.bump(sc.activation(qst[:], pp[:], AF.Identity,
                                             bias=bias, scale=ISCALE), "act")
                    elif kind == "k":
                        bias = P.sb_t["kb0_t"][:] if layer == 0 else 0.0
                        P.bump(sc.activation(qst[:], pp[:], AF.Identity,
                                             bias=bias, scale=1.0), "act")
                    else:
                        bias = P.sb_t["vb0_t"][:] if layer == 0 else 0.0
                        P.bump(sc.activation(qst[:], pp[:], AF.Identity,
                                             bias=bias, scale=1.0), "act")
                else:
                    if layer == 0:
                        P.bump(sc.activation(qst[:], pp[:], AF.Identity,
                                             bias=P.sb_t["ob0_t"][:, oc:oc + 1],
                                             scale=1.0), "act")
                    else:
                        P.bump(sc.copy(qst[:], pp[:]), "act")
                ev_done[i] = P.cnt["act"]

            if mode == "qkv" and kind in ("q", "k") and rope:
                @blk.vector
                def _(v, i=i):
                    qst = qst2[i % 2]
                    v.wait_ge(P.sems["act"], ev_done[i])
                    v.tensor_copy(qtmp[0:64, :], qst[64:128, :])
                    v.tensor_copy(qtmp[64:128, :], qst[0:64, :])
                    v.tensor_tensor(qtmp[:], qtmp[:], P.sb_t["sinb"][:], ALU.mult)
                    v.tensor_tensor(qst[:], qst[:], P.sb_t["cosb"][:], ALU.mult)
                    P.bump(v.tensor_tensor(qst[:], qst[:], qtmp[:], ALU.add),
                           "dve")
                    ep_done[i] = P.cnt["dve"]
            elif mode == "qkv" and kind == "v":
                @blk.tensor
                def _(t, i=i):
                    qst = qst2[i % 2]
                    t.wait_ge(P.sems["act"], ev_done[i])
                    if i >= 1 and tasks[i - 1][0] == "v":
                        t.wait_ge(P.sems["dve"], vtm_done[i - 1])
                    for c in range(16):
                        ins = t.transpose(ptr[:, c * 128:(c + 1) * 128],
                                          qst[:, c * 128:(c + 1) * 128],
                                          identb[:])
                        if c == 15:
                            P.bump(ins, "pe")
                    tr_done[i] = P.cnt["pe"]

                @blk.vector
                def _(v, i=i):
                    v.wait_ge(P.sems["pe"], tr_done[i])
                    if i >= 1 and tasks[i - 1][0] == "v":
                        v.wait_ge(P.sems["qs"], st_done[i - 1])
                    P.bump(v.tensor_copy(vtm[:], ptr[:]), "dve")
                    vtm_done[i] = P.cnt["dve"]
            elif mode == "oproj":
                @blk.vector
                def _(v, i=i):
                    v.wait_ge(P.sems["act"], ev_done[i])
                    v.wait_ge(P.sems["qs"], ld_done[("r", i)])
                    if i >= 1:
                        v.wait_ge(P.sems["qs"], st_done[i - 1])
                    P.bump(v.tensor_tensor(rst[:], qst2[i % 2][:], resk[:],
                                           ALU.add), "dve")
                    ep_done[i] = P.cnt["dve"]

            @blk.sync
            def _(s, i=i, kind=kind, dst=dst, row=row):
                if mode == "qkv" and kind == "v":
                    s.wait_ge(P.sems["dve"], vtm_done[i])
                    P.dma(s, "qs", dst[:, row:row + 128].rearrange(
                        "(c p) hd -> p c hd", p=128),
                        vtm[:].rearrange("p (c hd) -> p c hd", c=16))
                elif mode == "qkv":
                    if rope and kind in ("q", "k"):
                        s.wait_ge(P.sems["dve"], ep_done[i])
                    else:
                        s.wait_ge(P.sems["act"], ev_done[i])
                    P.dma(s, "qs", dst[row:row + 128, :], qst2[i % 2][:])
                else:
                    s.wait_ge(P.sems["dve"], ep_done[i])
                    P.dma(s, "qs", dst[row:row + 128, :], rst[:])
                st_done[i] = P.cnt["qs"]

        @blk.sync
        def _(s):
            s.wait_ge(P.sems["qs"], P.cnt["qs"])



def emit_gather(P, layer):
    kvh = 1 if layer == 0 else 8
    ko = P.dram(f"k_own{layer}", [kvh * 128, TOK], BF16, io="x")
    vo = P.dram(f"v_own{layer}", [TOK, kvh * 128], BF16, io="x")
    kf = P.dram(f"k_full{layer}", [2 * kvh * 128, TOK], BF16, io="x")
    vf = P.dram(f"v_full{layer}", [2 * TOK, kvh * 128], BF16, io="x")
    groups = [[0, 1], [2, 3], [4, 5], [6, 7]]
    with P.block_guard() as blk:
        @blk.gpsimd
        def _(g):
            g.collective_compute(
                "AllGather", ALU.bypass, replica_groups=groups,
                ins=[ko.opt()], outs=[kf.opt()]).then_inc(P.sems["cc"], 1)
            P.cnt["cc"] += 1
            g.collective_compute(
                "AllGather", ALU.bypass, replica_groups=groups,
                ins=[vo.opt()], outs=[vf.opt()]).then_inc(P.sems["cc"], 1)
            P.cnt["cc"] += 1
            g.wait_ge(P.sems["cc"], P.cnt["cc"])



def emit_attn(P, layer):
    nc = P.nc
    kvh = 1 if layer == 0 else 8
    qd = P.dram("qd", [D, TOK], BF16, io="x")
    kf = P.dram(f"k_full{layer}", [2 * kvh * 128, TOK], BF16, io="x")
    vf = P.dram(f"v_full{layer}", [2 * TOK, kvh * 128], BF16, io="x")
    slot = P.sb_t["slot"]
    mb = P.sb_t["mb"]
    identb = P.sb_t["identb"]
    kbuf = P.sbl("kbuf", [128, T], BF16)
    vbuf = P.sbl("vbuf", [128, 32 * 128], BF16)
    qbuf = [P.sbl("qbuf0", [128, TOK], BF16), P.sbl("qbuf1", [128, TOK], BF16)]
    pbufs = [P.sbl("pbuf0", [128, 1024], BF16), P.sbl("pbuf1", [128, 1024], BF16)]
    ptb = [P.sbl("ptb0", [128, 1024], BF16), P.sbl("ptb1", [128, 1024], BF16)]
    nmb = [P.sbl("nmb0", [128, 1], F32), P.sbl("nmb1", [128, 1], F32)]
    mtmp = [P.sbl("mtmp0", [128, 1], F32), P.sbl("mtmp1", [128, 1], F32)]
    lrun = [P.sbl("lrun0", [128, 1], F32), P.sbl("lrun1", [128, 1], F32)]
    lpart = [P.sbl("lpart0", [128, 1], F32), P.sbl("lpart1", [128, 1], F32)]
    linv = [P.sbl("linv0", [128, 1], F32), P.sbl("linv1", [128, 1], F32)]
    ctm = [P.sbl("ctm0", [128, 128], BF16), P.sbl("ctm1", [128, 128], BF16)]

    with P.block_guard() as blk, ExitStack() as ps:
        psc = [P.st_psum(ps, f"psc{j}_{layer}", [128, 1024], F32)
               for j in range(2)]
        ppt = [P.st_psum(ps, f"ppt{j}_{layer}", [128, 1024], BF16)
               for j in range(2)]
        bankT = [P.st_psum(ps, f"bkt{j}_{layer}", [128, 1024], BF16)
                 for j in range(2)]
        pav = [bankT[j][:, 0:256].bitcast(F32) for j in range(2)]
        pct = [bankT[j][:, 512:640] for j in range(2)]

        kv_ld = {}
        q_ld = {}
        heads = [(h, h // (NH // kvh)) for h in range(NH)]
        pr = {"psc_exp": [0, 0],
              "pav_rd": [0, 0],
              "pct_rd": [0, 0],
              "nm_rd": [0, 0],
              "stat_rd": [0, 0],
              "unit": 0,
              "ppt_rd": [0, 0],
              "qk_tail": {}}

        def emit_kv_load(g, kv):
            for gblk in range(8):
                r, pos = OWNER[gblk], POS[gblk]
                P.dma(g, "qs",
                      kbuf[:, gblk * 512:(gblk + 1) * 512],
                      kf[r * kvh * 128 + kv * 128:
                         r * kvh * 128 + (kv + 1) * 128,
                         pos * 512:(pos + 1) * 512])
                P.dma(g, "qs",
                      vbuf[:].rearrange("p (c hd) -> p c hd", c=32)
                      [:, gblk * 4:(gblk + 1) * 4, :],
                      vf[r * TOK + pos * 512: r * TOK + (pos + 1) * 512,
                         kv * 128:(kv + 1) * 128]
                      .rearrange("(c p) hd -> p c hd", p=128))
            kv_ld[kv] = P.cnt["qs"]

        @blk.scalar
        def _(sc):
            P.preload_tables(sc)

        @blk.sync
        def _(g):
            emit_kv_load(g, 0)

        @blk.sync
        def _(s):
            P.dma(s, "qs", qbuf[0][:], qd[heads[0][0] * 128:
                                          heads[0][0] * 128 + 128, :])
            q_ld[0] = P.cnt["qs"]

        for hi, (h, kv) in enumerate(heads):
            @blk.sync
            def _(s, hi=hi):
                if hi + 1 < NH:
                    if hi >= 1:
                        s.wait_ge(P.sems["pe"], pr["qk_tail"][hi - 1])
                    hn = heads[hi + 1][0]
                    P.dma(s, "qs", qbuf[(hi + 1) % 2][:],
                          qd[hn * 128:hn * 128 + 128, :])
                    q_ld[hi + 1] = P.cnt["qs"]

            for lb in range(4):
                for qc in range(4):
                    up = pr["unit"] % 2
                    pr["unit"] += 1
                    emit_attn_qc(P, blk, hi, kv, h, lb, qc, SP[lb] // 2,
                                 psc, ppt, pav[up], pct[up], kbuf, vbuf,
                                 qbuf, pbufs, ptb, nmb[up], mtmp[up],
                                 lrun[up], lpart[up], linv[up], ctm[up], mb,
                                 identb, slot, kv_ld, q_ld, pr, up,
                                 pr["unit"] - 1)
            pr["qk_tail"][hi] = P.cnt["pe"]

            if hi + 1 < NH and heads[hi + 1][1] != kv:
                @blk.sync
                def _(g, kv=kv):
                    g.wait_ge(P.sems["pe"], P.cnt["pe"])
                    emit_kv_load(g, kv + 1)

        @blk.vector
        def _(v):
            v.wait_ge(P.sems["dve"], P.cnt["dve"])
            v.nop()


def emit_attn_qc(P, blk, hi, kv, h, lb, qc, nkb, psc, ppt, pav, pct,
                 kbuf, vbuf, qbuf, pbufs, ptb, nmb, mtmp, lrun, lpart, linv,
                 ctm, mb, identb, slot, kv_ld, q_ld, pr, up, uo):
    kbx = lambda kb: (kb + uo) % 2
    qsl = qbuf[hi % 2][:, lb * 512 + qc * 128: lb * 512 + qc * 128 + 128]
    mext_sl = mb[:].rearrange("p (gr w) -> p gr w", gr=2)[
        :, lb % 2, 384 - 128 * qc: 384 - 128 * qc + 1024]

    p1_mm = {}
    p1_red = {}
    for kb in range(nkb):
        @blk.tensor
        def _(t, kb=kb):
            if kb == 0:
                t.wait_ge(P.sems["qs"], kv_ld[kv])
                t.wait_ge(P.sems["qs"], q_ld[hi])
            if kb < 2:
                t.wait_ge(P.sems["act"], pr["psc_exp"][kbx(kb)])
            else:
                t.wait_ge(P.sems["dve"], p1_red[kb - 2])
            ins = None
            for half in range(2):
                ins = t.matmul(psc[kbx(kb)][:, half * 512:(half + 1) * 512],
                               qsl,
                               kbuf[:, kb * 1024 + half * 512:
                                    kb * 1024 + (half + 1) * 512],
                               start=True, stop=True)
            P.bump(ins, "pe")
            p1_mm[kb] = P.cnt["pe"]

        @blk.vector
        def _(v, kb=kb):
            v.wait_ge(P.sems["pe"], p1_mm[kb])
            if kb == 0:
                v.wait_ge(P.sems["act"], pr["nm_rd"][up])
            if kb == nkb - 1:
                v.tensor_tensor(psc[kbx(kb)][:], psc[kbx(kb)][:], mext_sl,
                                ALU.add)
            v.tensor_reduce(lpart[:], psc[kbx(kb)][:, 0:512], AX.X, ALU.max,
                            negate=True)
            v.tensor_reduce(mtmp[:], psc[kbx(kb)][:, 512:1024], AX.X, ALU.max,
                            negate=True)
            v.tensor_tensor(lpart[:], lpart[:], mtmp[:], ALU.min)
            if kb == 0:
                P.bump(v.tensor_copy(nmb[:], lpart[:]), "dve")
            else:
                P.bump(v.tensor_tensor(nmb[:], nmb[:], lpart[:], ALU.min),
                       "dve")
            p1_red[kb] = P.cnt["dve"]
    nm_done = P.cnt["dve"]

    p2_mm = {}
    p2_msk = {}
    p2_exp = {}
    p2_tr = {}
    p2_ev = {}
    av_done = {}
    l_upd = {}
    for kb in range(nkb):
        @blk.tensor
        def _(t, kb=kb):
            if kb == 0:
                t.wait_ge(P.sems["dve"], nm_done)
            if kb >= 2:
                t.wait_ge(P.sems["act"], p2_exp[kb - 2])
            ins = None
            for half in range(2):
                ins = t.matmul(psc[kbx(kb)][:, half * 512:(half + 1) * 512],
                               qsl,
                               kbuf[:, kb * 1024 + half * 512:
                                    kb * 1024 + (half + 1) * 512],
                               start=True, stop=True)
            P.bump(ins, "pe")
            p2_mm[kb] = P.cnt["pe"]

        if kb == nkb - 1:
            @blk.vector
            def _(v, kb=kb):
                v.wait_ge(P.sems["pe"], p2_mm[kb])
                P.bump(v.tensor_tensor(psc[kbx(kb)][:], psc[kbx(kb)][:],
                                       mext_sl, ALU.add), "dve")
                p2_msk[kb] = P.cnt["dve"]

        @blk.scalar
        def _(sc, kb=kb):
            if kb in p2_msk:
                sc.wait_ge(P.sems["dve"], p2_msk[kb])
            else:
                sc.wait_ge(P.sems["pe"], p2_mm[kb])
            if kb >= 2:
                sc.wait_ge(P.sems["pe"], p2_tr[kb - 2])
            if kb >= 1:
                sc.wait_ge(P.sems["dve"], l_upd[kb - 1])
            P.bump(sc.activation(pbufs[kbx(kb)][:], psc[kbx(kb)][:], AF.Exp,
                                 bias=nmb[:], scale=1.0,
                                 accum_out=lpart[:]), "act")
            p2_exp[kb] = P.cnt["act"]
            pr["psc_exp"][kbx(kb)] = P.cnt["act"]

        @blk.vector
        def _(v, kb=kb):
            v.wait_ge(P.sems["act"], p2_exp[kb])
            if kb == 0:
                P.bump(v.tensor_copy(lrun[:], lpart[:]), "dve")
            else:
                P.bump(v.tensor_tensor(lrun[:], lrun[:], lpart[:], ALU.add),
                       "dve")
            l_upd[kb] = P.cnt["dve"]

        @blk.tensor
        def _(t, kb=kb):
            t.wait_ge(P.sems["act"], p2_exp[kb])
            if kb >= 2:
                t.wait_ge(P.sems["dve"], p2_ev[kb - 2])
            else:
                t.wait_ge(P.sems["dve"], pr["ppt_rd"][kbx(kb)])
            ins = None
            for c in range(8):
                ins = t.transpose(ppt[kbx(kb)][:, c * 128:(c + 1) * 128],
                                  pbufs[kbx(kb)][:, c * 128:(c + 1) * 128],
                                  identb[:])
            P.bump(ins, "pe")
            p2_tr[kb] = P.cnt["pe"]

        @blk.vector
        def _(v, kb=kb):
            v.wait_ge(P.sems["pe"], p2_tr[kb])
            if kb >= 2:
                v.wait_ge(P.sems["pe"], av_done[kb - 2])
            P.bump(v.tensor_copy(ptb[kbx(kb)][:], ppt[kbx(kb)][:]), "dve")
            p2_ev[kb] = P.cnt["dve"]
            pr["ppt_rd"][kbx(kb)] = P.cnt["dve"]

        @blk.tensor
        def _(t, kb=kb):
            t.wait_ge(P.sems["dve"], p2_ev[kb])
            if kb == 0:
                t.wait_ge(P.sems["dve"], pr["pav_rd"][up])
                t.wait_ge(P.sems["dve"], pr["pct_rd"][up])
            ins = None
            for c in range(8):
                ins = t.matmul(pav,
                               ptb[kbx(kb)][:, c * 128:(c + 1) * 128],
                               vbuf[:, (kb * 8 + c) * 128:
                                    (kb * 8 + c + 1) * 128],
                               start=(kb == 0 and c == 0),
                               stop=(kb == nkb - 1 and c == 7))
            P.bump(ins, "pe")
            av_done[kb] = P.cnt["pe"]
    av_all = P.cnt["pe"]
    l_all = P.cnt["dve"]

    @blk.scalar
    def _(sc):
        sc.wait_ge(P.sems["dve"], l_all)
        sc.wait_ge(P.sems["dve"], pr["stat_rd"][up])
        sc.activation(linv[:], lrun[:], AF.Ln, bias=0.0, scale=1.0)
        P.bump(sc.activation(linv[:], linv[:], AF.Exp, bias=0.0,
                             scale=-1.0), "act")
    linv_done = P.cnt["act"]
    pr["nm_rd"][up] = P.cnt["act"]

    @blk.vector
    def _(v):
        v.wait_ge(P.sems["pe"], av_all)
        v.wait_ge(P.sems["act"], linv_done)
        P.bump(v.tensor_scalar(ctm[:], pav, linv[:], None, ALU.mult),
               "dve")
    ctm_done = P.cnt["dve"]
    pr["pav_rd"][up] = ctm_done
    pr["stat_rd"][up] = ctm_done

    @blk.tensor
    def _(t):
        t.wait_ge(P.sems["dve"], ctm_done)
        t.wait_ge(P.sems["dve"], pr["pct_rd"][up])
        P.bump(t.transpose(pct, ctm[:], identb[:]), "pe")
    ctr_done = P.cnt["pe"]

    @blk.vector
    def _(v):
        v.wait_ge(P.sems["pe"], ctr_done)
        P.bump(v.tensor_copy(
            slot[:, h * TOK + lb * 512 + qc * 128:
                 h * TOK + lb * 512 + qc * 128 + 128], pct), "dve")
    pr["pct_rd"][up] = P.cnt["dve"]



def emit_ln(P, final=False):
    nc = P.nc
    rbuf = P.dram("rbuf", [D, TOK], BF16, io="x")
    slot = P.sb_t["slot"]
    onescb = P.sb_t["onescb"]
    ones1f = P.sb_t["ones1f"]
    rck = [P.sbl("rck0", [128, TOK], BF16), P.sbl("rck1", [128, TOK], BF16)]
    rsq = P.sbl("rsq", [128, TOK], BF16)
    arow = P.sbl("arow", [1, TOK], F32)
    brow = P.sbl("brow", [1, TOK], F32)
    crow = P.sbl("crow", [1, TOK], F32)
    abuf = P.sbl("abuf", [128, TOK], BF16)
    bbuf = P.sbl("bbuf", [128, TOK], BF16)
    if final:
        yout = P.dram("y", [D, TOK], F32, io="out")
        yst = [P.sbl("yst0", [128, TOK], F32), P.sbl("yst1", [128, TOK], F32)]
    else:
        xres = P.dram("xres", [D, TOK], BF16, io="x")

    with P.block_guard() as blk, ExitStack() as ps:
        psum_s = P.st_psum(ps, "ps_lnsum", [1, TOK], F32)
        psum_q = P.st_psum(ps, "ps_lnsq", [1, TOK], F32)

        @blk.scalar
        def _(sc):
            P.preload_tables(sc)
        ld = {}
        sq = {}
        mm_done = {}
        for c in range(NCH):
            @blk.sync
            def _(s, c=c):
                if c >= 2:
                    s.wait_ge(P.sems["pe"], mm_done[c - 2])
                P.dma(s, "qs", rck[c % 2][:], rbuf[c * 128:(c + 1) * 128, :])
                ld[c] = P.cnt["qs"]

            @blk.vector
            def _(v, c=c):
                v.wait_ge(P.sems["qs"], ld[c])
                if c >= 1:
                    v.wait_ge(P.sems["pe"], mm_done[c - 1])
                P.bump(v.tensor_tensor(rsq[:], rck[c % 2][:], rck[c % 2][:],
                                       ALU.mult), "dve")
                sq[c] = P.cnt["dve"]

            @blk.tensor
            def _(t, c=c):
                t.wait_ge(P.sems["dve"], sq[c])
                ins = None
                for tt in range(4):
                    t.matmul(psum_s[0:1, tt * 512:(tt + 1) * 512], onescb[:],
                             rck[c % 2][:, tt * 512:(tt + 1) * 512],
                             start=(c == 0), stop=(c == NCH - 1))
                    ins = t.matmul(psum_q[0:1, tt * 512:(tt + 1) * 512],
                                   onescb[:], rsq[:, tt * 512:(tt + 1) * 512],
                                   start=(c == 0), stop=(c == NCH - 1))
                P.bump(ins, "pe")
                mm_done[c] = P.cnt["pe"]
        stats_done = P.cnt["pe"]

        @blk.vector
        def _(v):
            v.wait_ge(P.sems["pe"], stats_done)
            v.tensor_scalar(arow[:], psum_s[:], 1.0 / D, None, ALU.mult)
            v.tensor_scalar(brow[:], psum_q[:], 1.0 / D, None, ALU.mult)
            v.tensor_tensor(crow[:], arow[:], arow[:], ALU.mult)
            P.bump(v.tensor_tensor(brow[:], brow[:], crow[:], ALU.subtract),
                   "dve")
        var_done = P.cnt["dve"]

        @blk.vector
        def _(v):
            P.bump(v.tensor_scalar(brow[:], brow[:], float(LN_EPS), None,
                                   ALU.add), "dve")
        vare_done = P.cnt["dve"]

        @blk.scalar
        def _(sc):
            sc.wait_ge(P.sems["dve"], vare_done)
            sc.activation(brow[:], brow[:], AF.Ln, bias=0.0, scale=1.0)
            P.bump(sc.activation(brow[:], brow[:], AF.Exp, bias=0.0,
                                 scale=-0.5), "act")
        rstd_done = P.cnt["act"]

        @blk.vector
        def _(v):
            v.wait_ge(P.sems["act"], rstd_done)
            v.tensor_tensor(arow[:], arow[:], brow[:], ALU.mult)
            P.bump(v.tensor_scalar(arow[:], arow[:], -1.0, None, ALU.mult),
                   "dve")
        ab_done = P.cnt["dve"]

    with P.block_guard() as blk, ExitStack() as ps:
        pb = [P.st_psum(ps, f"ps_lnb{j}", [128, 512], F32) for j in range(2)]
        bc_done = {}
        bcmm = {}
        for tt in range(4):
            @blk.tensor
            def _(t, tt=tt):
                if tt == 0:
                    t.wait_ge(P.sems["dve"], ab_done)
                else:
                    t.wait_ge(P.sems["dve"], bc_done[tt - 1])
                t.matmul(pb[0][:], ones1f[:],
                         brow[0:1, tt * 512:(tt + 1) * 512],
                         start=True, stop=True)
                P.bump(t.matmul(pb[1][:], ones1f[:],
                                arow[0:1, tt * 512:(tt + 1) * 512],
                                start=True, stop=True), "pe")
                bcmm[tt] = P.cnt["pe"]

            @blk.vector
            def _(v, tt=tt):
                v.wait_ge(P.sems["pe"], bcmm[tt])
                v.tensor_copy(abuf[:, tt * 512:(tt + 1) * 512], pb[0][:])
                P.bump(v.tensor_copy(bbuf[:, tt * 512:(tt + 1) * 512],
                                     pb[1][:]), "dve")
                bc_done[tt] = P.cnt["dve"]
        bc_all = P.cnt["dve"]

        ld = {}
        ap_done = {}
        st_done = {}
        for c in range(NCH):
            @blk.sync
            def _(s, c=c):
                if c >= 2:
                    s.wait_ge(P.sems["dve"], ap_done[c - 2])
                P.dma(s, "qs", rck[c % 2][:], rbuf[c * 128:(c + 1) * 128, :])
                ld[c] = P.cnt["qs"]

            @blk.vector
            def _(v, c=c):
                v.wait_ge(P.sems["qs"], ld[c])
                if c == 0:
                    v.wait_ge(P.sems["dve"], bc_all)
                if final:
                    if c >= 2:
                        v.wait_ge(P.sems["qs"], st_done[c - 2])
                    v.tensor_tensor(rsq[:], rck[c % 2][:], abuf[:], ALU.mult)
                    P.bump(v.tensor_tensor(yst[c % 2][:], rsq[:], bbuf[:],
                                           ALU.add), "dve")
                else:
                    v.tensor_tensor(rsq[:], rck[c % 2][:], abuf[:], ALU.mult)
                    P.bump(v.tensor_tensor(
                        slot[:, c * TOK:(c + 1) * TOK], rsq[:], bbuf[:],
                        ALU.add), "dve")
                ap_done[c] = P.cnt["dve"]

            @blk.sync
            def _(s, c=c):
                s.wait_ge(P.sems["dve"], ap_done[c])
                if final:
                    P.dma(s, "qs", P.dram_t["y"][c * 128:(c + 1) * 128, :],
                          yst[c % 2][:])
                else:
                    P.dma(s, "qs",
                          P.dram_t["xres"][c * 128:(c + 1) * 128, :],
                          slot[:, c * TOK:(c + 1) * TOK])
                st_done[c] = P.cnt["qs"]

        @blk.sync
        def _(s):
            s.wait_ge(P.sems["qs"], P.cnt["qs"])



def emit_ffn0(P):
    nc = P.nc
    slot = P.sb_t["slot"]
    rbuf = P.dram("rbuf", [D, TOK], BF16, io="x")
    sg_d = P.dram("sg0", [32, 128, 32, 128], F32, io="in")
    wbuf = [P.sbl("wbuf0", [128, 4096], BF16), P.sbl("wbuf1", [128, 4096], BF16)]
    qst = P.sbl("qst", [128, TOK], BF16)
    rst = P.sbl("rst", [128, TOK], BF16)

    with P.block_guard() as blk, ExitStack() as ps:
        pp = P.st_psum(ps, "pp_ffn0", [128, TOK], F32)
        ld, mm_done, ev_done, ep_done, st_done = {}, {}, {}, {}, {}
        for i in range(32):
            @blk.gpsimd
            def _(g, i=i):
                def load(j):
                    if j >= 2:
                        g.wait_ge(P.sems["pe"], mm_done[j - 2])
                    P.dma(g, "qg", wbuf[j % 2][:],
                          sg_d[j].rearrange("ip ic op -> ip (ic op)"))
                    ld[j] = P.cnt["qg"]
                if i == 0:
                    load(0)
                    load(1)
                elif i + 1 < 32:
                    load(i + 1)

            @blk.tensor
            def _(t, i=i):
                t.wait_ge(P.sems["qg"], ld[i])
                if i >= 1:
                    t.wait_ge(P.sems["act"], ev_done[i - 1])
                for ic in range(NCH):
                    for tt in range(4):
                        ins = t.matmul(
                            pp[:, tt * 512:(tt + 1) * 512],
                            wbuf[i % 2][:, ic * 128:(ic + 1) * 128],
                            slot[:, ic * TOK + tt * 512:ic * TOK + (tt + 1) * 512],
                            start=(ic == 0), stop=(ic == NCH - 1))
                P.bump(ins, "pe")
                mm_done[i] = P.cnt["pe"]

            @blk.scalar
            def _(sc, i=i):
                sc.wait_ge(P.sems["pe"], mm_done[i])
                if i >= 1:
                    sc.wait_ge(P.sems["dve"], ep_done[i - 1])
                P.bump(sc.activation(qst[:], pp[:], AF.Identity,
                                     bias=P.sb_t["fb0_t"][:, i:i + 1],
                                     scale=P.sb_t["fs0_t"][:, i:i + 1]), "act")
                ev_done[i] = P.cnt["act"]

            @blk.vector
            def _(v, i=i):
                v.wait_ge(P.sems["act"], ev_done[i])
                if i >= 1:
                    v.wait_ge(P.sems["qs"], st_done[i - 1])
                P.bump(v.tensor_tensor(rst[:], qst[:],
                                       slot[:, i * TOK:(i + 1) * TOK],
                                       ALU.add), "dve")
                ep_done[i] = P.cnt["dve"]

            @blk.sync
            def _(s, i=i):
                s.wait_ge(P.sems["dve"], ep_done[i])
                P.dma(s, "qs", rbuf[i * 128:(i + 1) * 128, :], rst[:])
                st_done[i] = P.cnt["qs"]

        @blk.sync
        def _(s):
            s.wait_ge(P.sems["qs"], P.cnt["qs"])



def emit_ffn1(P):
    nc = P.nc
    slot = P.sb_t["slot"]
    rbuf = P.dram("rbuf", [D, TOK], BF16, io="x")
    wall_d = P.dram("wall", [128, 32, 24], F32, io="in")
    gate_d = P.dram("gate", [1, 1], F32, io="in")
    wbr_d = P.dram("wbr", [8, 128, 8, 128], F32, io="in")
    esel_d = P.dram("esel", [24, 24 * 128], BF16, io="in")
    ones1f = P.sb_t["ones1f"]
    ones1b = P.sb_t["ones1b"]
    onescb = P.sb_t["onescb"]
    identf = P.sb_t["identf"]
    b24t = P.sb_t["b24_t"]

    wallb = P.sbl("wallb", [128, 32 * 24], BF16)
    wbrw = [P.sbl("wbrw0", [128, 8 * 128], BF16),
            P.sbl("wbrw1", [128, 8 * 128], BF16)]
    gates = P.sbl("gates", [1, 8], F32)
    g128 = P.sbl("g128", [128, 1], F32)
    gm1 = P.sbl("gm1", [128, 1], F32)
    rsq = P.sbl("rsq", [128, TOK], BF16)
    srow = P.sbl("srow", [1, TOK], BF16)
    rsb = P.sbl("rsb", [128, TOK], BF16)
    logT = P.sbl("logT", [128, 16 * 24], F32)
    lst = P.sbl("lst", [24, 512], F32)
    esel = P.sbl("esel_b", [24, 24 * 128], BF16)
    Zb = P.sbl("Zb", [128, 256], F32)
    Tb = P.sbl("Tb", [128, 256], F32)
    Mb4 = P.sbl("Mb4", [128, 64], F32)
    ub = P.sbl("ub", [128, 64], F32)
    vb4 = P.sbl("vb4", [128, 64], F32)
    HR = P.sbl("HR", [24, TOK], BF16)
    Htm = P.sbl("Htm", [128, 16 * 24], F32)
    Hb = P.sbl("Hb", [128, 24 * 512], BF16)
    Xp = P.sbl("Xp", [128, 8 * 512], BF16)
    accb = [P.sbl("acc0", [128, 512], BF16), P.sbl("acc1", [128, 512], BF16)]
    tm1 = P.sbl("tm1", [128, 512], BF16)
    tm2 = P.sbl("tm2", [128, 512], BF16)

    Z4 = Zb[:].rearrange("p (s i j) -> p s i j", s=16, i=4)
    T4 = Tb[:].rearrange("p (s i j) -> p s i j", s=16, i=4)
    T4j = Tb[:].rearrange("p (s i j) -> p s j i", s=16, i=4)
    u3 = ub[:].rearrange("p (s i) -> p s i", s=16)
    v3 = vb4[:].rearrange("p (s j) -> p s j", s=16)
    M3 = Mb4[:, 0:64].rearrange("p (s i) -> p s i", s=16)

    with P.block_guard() as blk, ExitStack() as ps:
        psum_q = P.st_psum(ps, "ps_rms", [1, TOK], F32)
        p24 = P.st_psum(ps, "ps_l24", [24, 512], F32)
        pb = P.st_psum(ps, "ps_f1b", [128, 512], F32)
        ptp = P.st_psum(ps, "ps_f1t", [128, 128], F32)

        @blk.gpsimd
        def _(g):
            P.dma(g, "qg", wallb[:],
                  wall_d.rearrange("ip ic w -> ip (ic w)"))
            P.dma(g, "qg", gates[0:1, 0:1], gate_d)
            P.dma(g, "qg", esel[:], esel_d)
        wld = P.cnt["qg"]

        @blk.scalar
        def _(sc):
            P.preload_tables(sc)

        sq = {}
        mm_done = {}
        for c in range(NCH):
            @blk.vector
            def _(v, c=c):
                if c >= 1:
                    v.wait_ge(P.sems["pe"], mm_done[c - 1])
                P.bump(v.tensor_tensor(rsq[:], slot[:, c * TOK:(c + 1) * TOK],
                                       slot[:, c * TOK:(c + 1) * TOK],
                                       ALU.mult), "dve")
                sq[c] = P.cnt["dve"]

            @blk.tensor
            def _(t, c=c):
                t.wait_ge(P.sems["dve"], sq[c])
                ins = None
                for tt in range(4):
                    ins = t.matmul(psum_q[0:1, tt * 512:(tt + 1) * 512],
                                   onescb[:], rsq[:, tt * 512:(tt + 1) * 512],
                                   start=(c == 0), stop=(c == NCH - 1))
                P.bump(ins, "pe")
                mm_done[c] = P.cnt["pe"]
        rms_mm = P.cnt["pe"]

        @blk.vector
        def _(v):
            v.wait_ge(P.sems["pe"], rms_mm)
            P.bump(v.tensor_scalar(srow[:], psum_q[:], 1.0 / D,
                                   float(RMS_EPS), ALU.mult, ALU.add), "dve")
        ms_done = P.cnt["dve"]

        @blk.scalar
        def _(sc):
            sc.wait_ge(P.sems["dve"], ms_done)
            sc.activation(srow[:], srow[:], AF.Ln, bias=0.0, scale=1.0)
            P.bump(sc.activation(srow[:], srow[:], AF.Exp, bias=0.0,
                                 scale=-0.5), "act")
        rstd_done = P.cnt["act"]

        bcd = {}
        bcm = {}
        for tt in range(4):
            @blk.tensor
            def _(t, tt=tt):
                t.wait_ge(P.sems["act"], rstd_done)
                if tt >= 1:
                    t.wait_ge(P.sems["dve"], bcd[tt - 1])
                P.bump(t.matmul(pb[:], ones1b[:],
                                srow[0:1, tt * 512:(tt + 1) * 512],
                                start=True, stop=True), "pe")
                bcm[tt] = P.cnt["pe"]

            @blk.vector
            def _(v, tt=tt):
                v.wait_ge(P.sems["pe"], bcm[tt])
                P.bump(v.tensor_copy(rsb[:, tt * 512:(tt + 1) * 512], pb[:]),
                       "dve")
                bcd[tt] = P.cnt["dve"]
        rsb_done = P.cnt["dve"]

        lg_mm = {}
        lg_ev = {}
        tp_mm = {}
        tp_ev = {}
        for tt in range(4):
            @blk.tensor
            def _(t, tt=tt):
                t.wait_ge(P.sems["qg"], wld)
                if tt == 0:
                    t.wait_ge(P.sems["dve"], rsb_done)
                else:
                    t.wait_ge(P.sems["dve"], lg_ev[tt - 1])
                ins = None
                for ic in range(NCH):
                    ins = t.matmul(p24[:], wallb[:, ic * 24:(ic + 1) * 24],
                                   slot[:, ic * TOK + tt * 512:
                                        ic * TOK + (tt + 1) * 512],
                                   start=(ic == 0), stop=(ic == NCH - 1))
                P.bump(ins, "pe")
                lg_mm[tt] = P.cnt["pe"]

            @blk.vector
            def _(v, tt=tt):
                v.wait_ge(P.sems["pe"], lg_mm[tt])
                if tt >= 1:
                    v.wait_ge(P.sems["dve"], tp_ev[(tt - 1, 3)])
                v.tensor_tensor(lst[:], p24[:],
                                rsb[0:24, tt * 512:(tt + 1) * 512], ALU.mult)
                P.bump(v.tensor_scalar(lst[:], lst[:],
                                       b24t[:], None, ALU.add), "dve")
                lg_ev[tt] = P.cnt["dve"]

            for sb4 in range(4):
                @blk.tensor
                def _(t, tt=tt, sb4=sb4):
                    t.wait_ge(P.sems["dve"], lg_ev[tt])
                    prev = tp_ev.get((tt, sb4 - 1), tp_ev.get((tt - 1, 3)))
                    if prev:
                        t.wait_ge(P.sems["dve"], prev)
                    P.bump(t.transpose(ptp[:, 0:24],
                                       lst[:, sb4 * 128:(sb4 + 1) * 128],
                                       identf[0:24, 0:24]), "pe")
                    tp_mm[(tt, sb4)] = P.cnt["pe"]

                @blk.vector
                def _(v, tt=tt, sb4=sb4):
                    v.wait_ge(P.sems["pe"], tp_mm[(tt, sb4)])
                    P.bump(v.tensor_copy(
                        logT[:].rearrange("p (s w) -> p s w", s=16)
                        [:, tt * 4 + sb4, :], ptp[:, 0:24]), "dve")
                    tp_ev[(tt, sb4)] = P.cnt["dve"]
        log_done = P.cnt["dve"]

        @blk.vector
        def _(v):
            v.wait_ge(P.sems["qg"], wld)
            P.bump(v.memset(gates[0:1, 1:8], 0.0), "dve")
        gz_done = P.cnt["dve"]

        @blk.scalar
        def _(sc):
            sc.wait_ge(P.sems["dve"], gz_done)
            P.bump(sc.activation(gates[:], gates[:], AF.Exp, bias=0.0,
                                 scale=-1.0), "act")
        eg_done = P.cnt["act"]

        @blk.vector
        def _(v):
            v.wait_ge(P.sems["act"], eg_done)
            P.bump(v.tensor_scalar(gates[:], gates[:], 1.0, None, ALU.add),
                   "dve")
        sig1_done = P.cnt["dve"]

        @blk.scalar
        def _(sc):
            sc.wait_ge(P.sems["dve"], sig1_done)
            sc.activation(gates[:], gates[:], AF.Ln, bias=0.0, scale=1.0)
            P.bump(sc.activation(gates[:], gates[:], AF.Exp, bias=0.0,
                                 scale=-1.0), "act")
        sig_done = P.cnt["act"]

        @blk.tensor
        def _(t):
            t.wait_ge(P.sems["act"], sig_done)
            t.wait_ge(P.sems["dve"], log_done)
            P.bump(t.matmul(ptp[:, 0:1], ones1f[:], gates[0:1, 0:1],
                            start=True, stop=True), "pe")
        g_bc = P.cnt["pe"]

        @blk.vector
        def _(v):
            v.wait_ge(P.sems["pe"], g_bc)
            v.tensor_copy(g128[:], ptp[:, 0:1])
            v.tensor_scalar(gm1[:], ptp[:, 0:1], -1.0, 1.0, ALU.mult, ALU.add)
            v.tensor_scalar(
                Zb[:].rearrange("p (s w) -> p s w", s=16),
                logT[:].rearrange("p (s w) -> p s w", s=16)[:, :, 0:16],
                TAU_I, None, ALU.mult)
            v.memset(ub[:], 0.0)
            P.bump(v.memset(vb4[:], 0.0), "dve")
        z_done = P.cnt["dve"]

        tsub_done = {}
        exp_done = {}
        s_done = {}
        ln_done = {}
        for it in range(50):
            @blk.vector
            def _(v, it=it):
                if it == 0:
                    v.wait_ge(P.sems["dve"], z_done)
                else:
                    v.wait_ge(P.sems["act"], ln_done[(it - 1, 1)])
                    v.tensor_tensor(vb4[:], vb4[:], Mb4[:, 0:64], ALU.add)
                    v.tensor_scalar(vb4[:], vb4[:], -1.0, None, ALU.mult)
                v.tensor_tensor(T4, Z4, v3.unsqueeze(2).broadcast_to(
                    (128, 16, 4, 4)), ALU.add)
                v.tensor_reduce(M3, T4, AX.X, ALU.max)
                P.bump(v.tensor_tensor(T4, T4, M3.unsqueeze(3).broadcast_to(
                    (128, 16, 4, 4)), ALU.subtract), "dve")
                tsub_done[(it, 0)] = P.cnt["dve"]

            @blk.scalar
            def _(sc, it=it):
                sc.wait_ge(P.sems["dve"], tsub_done[(it, 0)])
                P.bump(sc.activation(Tb[:], Tb[:], AF.Exp), "act")
                exp_done[(it, 0)] = P.cnt["act"]

            @blk.vector
            def _(v, it=it):
                v.wait_ge(P.sems["act"], exp_done[(it, 0)])
                P.bump(v.tensor_reduce(u3, T4, AX.X, ALU.add), "dve")
                s_done[(it, 0)] = P.cnt["dve"]

            @blk.scalar
            def _(sc, it=it):
                sc.wait_ge(P.sems["dve"], s_done[(it, 0)])
                P.bump(sc.activation(ub[:], ub[:], AF.Ln), "act")
                ln_done[(it, 0)] = P.cnt["act"]

            @blk.vector
            def _(v, it=it):
                v.wait_ge(P.sems["act"], ln_done[(it, 0)])
                v.tensor_tensor(ub[:], ub[:], Mb4[:, 0:64], ALU.add)
                v.tensor_scalar(ub[:], ub[:], -1.0, None, ALU.mult)
                v.tensor_tensor(T4, Z4, u3.unsqueeze(3).broadcast_to(
                    (128, 16, 4, 4)), ALU.add)
                v.tensor_reduce(M3, T4j, AX.X, ALU.max)
                P.bump(v.tensor_tensor(
                    T4j, T4j, M3.unsqueeze(3).broadcast_to((128, 16, 4, 4)),
                    ALU.subtract), "dve")
                tsub_done[(it, 1)] = P.cnt["dve"]

            @blk.scalar
            def _(sc, it=it):
                sc.wait_ge(P.sems["dve"], tsub_done[(it, 1)])
                P.bump(sc.activation(Tb[:], Tb[:], AF.Exp), "act")
                exp_done[(it, 1)] = P.cnt["act"]

            @blk.vector
            def _(v, it=it):
                v.wait_ge(P.sems["act"], exp_done[(it, 1)])
                P.bump(v.tensor_reduce(v3, T4j, AX.X, ALU.add), "dve")
                s_done[(it, 1)] = P.cnt["dve"]

            @blk.scalar
            def _(sc, it=it):
                sc.wait_ge(P.sems["dve"], s_done[(it, 1)])
                P.bump(sc.activation(vb4[:], vb4[:], AF.Ln), "act")
                ln_done[(it, 1)] = P.cnt["act"]

        @blk.vector
        def _(v):
            v.wait_ge(P.sems["act"], ln_done[(49, 1)])
            v.tensor_tensor(vb4[:], vb4[:], Mb4[:, 0:64], ALU.add)
            v.tensor_scalar(vb4[:], vb4[:], -1.0, None, ALU.mult)
            v.tensor_tensor(T4, Z4, u3.unsqueeze(3).broadcast_to(
                (128, 16, 4, 4)), ALU.add)
            P.bump(v.tensor_tensor(T4, T4, v3.unsqueeze(2).broadcast_to(
                (128, 16, 4, 4)), ALU.add), "dve")
        h_arg = P.cnt["dve"]

        @blk.scalar
        def _(sc):
            sc.wait_ge(P.sems["dve"], h_arg)
            P.bump(sc.activation(Tb[:], Tb[:], AF.Exp), "act")
        h_exp = P.cnt["act"]

        @blk.vector
        def _(v):
            v.wait_ge(P.sems["act"], h_exp)
            v.tensor_scalar(Tb[:], Tb[:], g128[:], None, ALU.mult)
            v.tensor_scalar(
                Tb[:].rearrange("p (s w) -> p s w", s=16)[:, :, 0:16:5],
                Tb[:].rearrange("p (s w) -> p s w", s=16)[:, :, 0:16:5],
                gm1[:], None, ALU.add)
            v.tensor_copy(
                Htm[:].rearrange("p (s w) -> p s w", s=16, w=24)[:, :, 0:16],
                Tb[:].rearrange("p (s w) -> p s w", s=16, w=16))
            for off, dst in ((16, 16), (20, 20)):
                lg = logT[:].rearrange("p (s w) -> p s w", s=16, w=24)[
                    :, :, off:off + 4]
                h4 = Htm[:].rearrange("p (s w) -> p s w", s=16, w=24)[
                    :, :, dst:dst + 4]
                m2 = Mb4[:, 0:16].rearrange("p (s o) -> p s o", s=16)
                v.tensor_reduce(m2, lg, AX.X, ALU.max)
                v.tensor_tensor(h4, lg, m2.broadcast_to((128, 16, 4)),
                                ALU.subtract)
            P.bump(v.tensor_copy(Htm[0:1, 0:1], Htm[0:1, 0:1]), "dve")
        hp_sub = P.cnt["dve"]

        @blk.scalar
        def _(sc):
            sc.wait_ge(P.sems["dve"], hp_sub)
            P.bump(sc.activation(
                Htm[:].rearrange("p (s w) -> p s w", s=16, w=24)[:, :, 16:24],
                Htm[:].rearrange("p (s w) -> p s w", s=16, w=24)[:, :, 16:24],
                AF.Exp), "act")
        hp_exp = P.cnt["act"]

        @blk.vector
        def _(v):
            v.wait_ge(P.sems["act"], hp_exp)
            h8 = Htm[:].rearrange("p (s w) -> p s w", s=16, w=24)[
                :, :, 16:24].rearrange("p s (k o) -> p s k o", k=2)
            m8 = Mb4[:, 0:32].rearrange("p (s k) -> p s k", s=16).unsqueeze(3)
            P.bump(v.tensor_reduce(m8, h8, AX.X, ALU.add), "dve")
        hs_done = P.cnt["dve"]

        @blk.scalar
        def _(sc):
            sc.wait_ge(P.sems["dve"], hs_done)
            sc.activation(Mb4[:, 0:32], Mb4[:, 0:32], AF.Ln, bias=0.0,
                          scale=1.0)
            P.bump(sc.activation(Mb4[:, 0:32], Mb4[:, 0:32], AF.Exp, bias=0.0,
                                 scale=-1.0), "act")
        hsr_done = P.cnt["act"]

        @blk.vector
        def _(v):
            v.wait_ge(P.sems["act"], hsr_done)
            h8 = Htm[:].rearrange("p (s w) -> p s w", s=16, w=24)[
                :, :, 16:24].rearrange("p s (k o) -> p s k o", k=2)
            m8 = Mb4[:, 0:32].rearrange("p (s k) -> p s k", s=16).unsqueeze(3)
            v.tensor_tensor(h8, h8, m8.broadcast_to((128, 16, 2, 4)),
                            ALU.mult)
            P.bump(v.tensor_copy(Htm[0:1, 0:1], Htm[0:1, 0:1]), "dve")
        htm_done = P.cnt["dve"]

        tphm = {}
        tph = {}
        for s16 in range(16):
            @blk.tensor
            def _(t, s16=s16):
                t.wait_ge(P.sems["dve"], htm_done if s16 == 0
                          else tph[s16 - 1])
                P.bump(t.transpose(ptp[0:24, 0:128],
                                   Htm[:, s16 * 24:(s16 + 1) * 24],
                                   identf[:]), "pe")
                tphm[s16] = P.cnt["pe"]

            @blk.vector
            def _(v, s16=s16):
                v.wait_ge(P.sems["pe"], tphm[s16])
                P.bump(v.tensor_copy(HR[:, s16 * 128:(s16 + 1) * 128],
                                     ptp[0:24, 0:128]), "dve")
                tph[s16] = P.cnt["dve"]
        hr_done = P.cnt["dve"]

        if "dbg:ffn1" in P.externals:
            d_log = P.dram("d_log", [128, 384], F32, io="out")
            d_htm = P.dram("d_htm", [128, 384], F32, io="out")
            d_u = P.dram("d_u", [128, 64], F32, io="out")
            d_v = P.dram("d_v", [128, 64], F32, io="out")
            d_hr = P.dram("d_hr", [24, TOK], BF16, io="out")
            d_g = P.dram("d_g", [128, 1], F32, io="out")
            d_gm = P.dram("d_gm", [128, 1], F32, io="out")
            d_tb = P.dram("d_tb", [128, 256], F32, io="out")

            @blk.sync
            def _(s):
                s.wait_ge(P.sems["dve"], hr_done)
                P.dma(s, "qs", d_log, logT[:])
                P.dma(s, "qs", d_htm, Htm[:])
                P.dma(s, "qs", d_u, ub[:])
                P.dma(s, "qs", d_v, vb4[:])
                P.dma(s, "qs", d_hr, HR[0:24, :])
                P.dma(s, "qs", d_g, g128[:])
                P.dma(s, "qs", d_gm, gm1[:])
                P.dma(s, "qs", d_tb, Tb[:])
                s.wait_ge(P.sems["qs"], P.cnt["qs"])

    with P.block_guard() as blk, ExitStack() as ps:
        pbm = P.st_psum(ps, "ps_f1bm", [128, 512], F32)
        py = [P.st_psum(ps, f"ps_f1y{j}", [128, 512], F32) for j in range(2)]
        y_done = {}
        mix_done = {}
        st_done = {}
        bc_done = {}
        bc_mm = {}
        acc_free = {0: 0, 1: 0}
        wbr_ld = {}
        for tt in range(4):
            for r in range(24):
                @blk.tensor
                def _(t, tt=tt, r=r):
                    if tt == 0 and r == 0:
                        t.wait_ge(P.sems["dve"], hr_done)
                    else:
                        t.wait_ge(P.sems["dve"], bc_done[(tt, r - 1)]
                                  if r else bc_done[(tt - 1, 23)])
                    if r == 0 and tt >= 1:
                        t.wait_ge(P.sems["dve"],
                                  mix_done[(tt - 1, 7, 3)])
                    P.bump(t.matmul(pbm[:],
                                    esel[:, r * 128:(r + 1) * 128],
                                    HR[0:24, tt * 512:(tt + 1) * 512],
                                    start=True, stop=True), "pe")
                    bc_mm[(tt, r)] = P.cnt["pe"]

                @blk.vector
                def _(v, tt=tt, r=r):
                    v.wait_ge(P.sems["pe"], bc_mm[(tt, r)])
                    P.bump(v.tensor_copy(Hb[:, r * 512:(r + 1) * 512],
                                         pbm[:]), "dve")
                    bc_done[(tt, r)] = P.cnt["dve"]
            hb_done = P.cnt["dve"]

            @blk.vector
            def _(v, tt=tt, hb_done=hb_done):
                v.wait_ge(P.sems["dve"], hb_done)
                if tt >= 1:
                    v.wait_ge(P.sems["pe"], y_done[(tt - 1, 7)])
                for cc in range(8):
                    xp = Xp[:, cc * 512:(cc + 1) * 512]
                    v.tensor_tensor(
                        xp, slot[:, cc * TOK + tt * 512:
                                 cc * TOK + (tt + 1) * 512],
                        Hb[:, 16 * 512:17 * 512], ALU.mult)
                    for n in range(1, 4):
                        v.tensor_tensor(
                            tm1[:], slot[:, (n * 8 + cc) * TOK + tt * 512:
                                         (n * 8 + cc) * TOK + (tt + 1) * 512],
                            Hb[:, (16 + n) * 512:(17 + n) * 512], ALU.mult)
                        v.tensor_tensor(xp, xp, tm1[:], ALU.add)
                P.bump(v.tensor_copy(Xp[0:1, 0:1], Xp[0:1, 0:1]), "dve")
            xp_done = P.cnt["dve"]

            for ocp in range(8):
                @blk.gpsimd
                def _(g, tt=tt, ocp=ocp):
                    pk = (tt, ocp - 2) if ocp >= 2 else \
                        ((tt - 1, ocp + 6) if tt >= 1 else None)
                    if pk is not None and pk in y_done:
                        g.wait_ge(P.sems["pe"], y_done[pk])
                    P.dma(g, "qg", wbrw[ocp % 2][:],
                          wbr_d[ocp].rearrange("ip ic op -> ip (ic op)"))
                    wbr_ld[(tt, ocp)] = P.cnt["qg"]

                @blk.tensor
                def _(t, tt=tt, ocp=ocp, xp_done=xp_done):
                    t.wait_ge(P.sems["dve"], xp_done)
                    t.wait_ge(P.sems["qg"], wbr_ld[(tt, ocp)])
                    pk = ((tt, ocp - 2, 3) if ocp >= 2 else
                          ((tt - 1, ocp + 6, 3) if tt >= 1 else None))
                    if pk is not None and pk in mix_done:
                        t.wait_ge(P.sems["dve"], mix_done[pk])
                    ins = None
                    for icp in range(8):
                        ins = t.matmul(
                            py[ocp % 2][:],
                            wbrw[ocp % 2][:, icp * 128:(icp + 1) * 128],
                            Xp[:, icp * 512:(icp + 1) * 512],
                            start=(icp == 0), stop=(icp == 7))
                    P.bump(ins, "pe")
                    y_done[(tt, ocp)] = P.cnt["pe"]

                for i in range(4):
                    @blk.vector
                    def _(v, tt=tt, ocp=ocp, i=i):
                        if i == 0:
                            v.wait_ge(P.sems["pe"], y_done[(tt, ocp)])
                        if acc_free[i % 2]:
                            v.wait_ge(P.sems["qs"], acc_free[i % 2])
                        ch = i * 8 + ocp
                        a = accb[i % 2]
                        v.tensor_tensor(
                            a[:], slot[:, ocp * TOK + tt * 512:
                                       ocp * TOK + (tt + 1) * 512],
                            Hb[:, (i * 4) * 512:(i * 4 + 1) * 512], ALU.mult)
                        for j in range(1, 4):
                            v.tensor_tensor(
                                tm1[:],
                                slot[:, (j * 8 + ocp) * TOK + tt * 512:
                                     (j * 8 + ocp) * TOK + (tt + 1) * 512],
                                Hb[:, (i * 4 + j) * 512:(i * 4 + j + 1) * 512],
                                ALU.mult)
                            v.tensor_tensor(a[:], a[:], tm1[:], ALU.add)
                        v.tensor_tensor(tm2[:], py[ocp % 2][:],
                                        Hb[:, (20 + i) * 512:(21 + i) * 512],
                                        ALU.mult)
                        v.tensor_tensor(a[:], a[:], tm2[:], ALU.add)
                        P.bump(v.tensor_tensor(
                            a[:], a[:], slot[:, ch * TOK + tt * 512:
                                             ch * TOK + (tt + 1) * 512],
                            ALU.add), "dve")
                        mix_done[(tt, ocp, i)] = P.cnt["dve"]

                    @blk.sync
                    def _(s, tt=tt, ocp=ocp, i=i):
                        s.wait_ge(P.sems["dve"], mix_done[(tt, ocp, i)])
                        ch = i * 8 + ocp
                        P.dma(s, "qs",
                              rbuf[ch * 128:(ch + 1) * 128,
                                   tt * 512:(tt + 1) * 512], accb[i % 2][:])
                        st_done[(tt, ocp, i)] = P.cnt["qs"]
                        acc_free[i % 2] = P.cnt["qs"]

        @blk.sync
        def _(s):
            s.wait_ge(P.sems["qs"], P.cnt["qs"])



PHASES_ALL = ["init", "qkv0", "gather0", "attn0", "oproj0", "ln1",
              "ffn0", "ln2", "qkv1", "gather1", "attn1", "oproj1", "ln3",
              "ffn1", "ln4"]


def build_program(phases, externals=()):
    P = Prog(externals=externals)
    for pi, ph in enumerate(phases):
        P._phase_idx = pi
        P._local_st = ExitStack()
        P._local_names = []
        if ph == "init":
            emit_init(P)
        elif ph == "initc":
            emit_init(P, with_x=False)
        elif ph == "qkv0":
            emit_proj(P, 0, "qkv")
        elif ph == "qkv1":
            emit_proj(P, 1, "qkv")
        elif ph == "gather0":
            emit_gather(P, 0)
        elif ph == "gather1":
            emit_gather(P, 1)
        elif ph == "attn0":
            emit_attn(P, 0)
        elif ph == "attn1":
            emit_attn(P, 1)
        elif ph == "oproj0":
            emit_proj(P, 0, "oproj")
        elif ph == "oproj1":
            emit_proj(P, 1, "oproj")
        elif ph in ("ln1", "ln2", "ln3"):
            emit_ln(P, final=False)
        elif ph == "ln4":
            emit_ln(P, final=True)
        elif ph == "ffn0":
            emit_ffn0(P)
        elif ph == "ffn1":
            emit_ffn1(P)
        elif ph == "loadslot":
            emit_loadslot(P)
        elif ph == "saveslot":
            emit_saveslot(P)
        elif ph == "saveslot2":
            emit_saveslot(P, "slotdram2")
        else:
            raise ValueError(ph)
        for n in P._local_names:
            del P.sb_t[n]
        P._local_st.close()
    return P


def emit_loadslot(P):
    slot = P.sb("slot", [128, NCH * TOK], BF16)
    sdr = P.dram("slotdram", [D, TOK], BF16, io="x")
    with P.block_guard() as blk:
        @blk.sync
        def _(s):
            for c in range(NCH):
                P.dma(s, "qs", slot[:, c * TOK:(c + 1) * TOK],
                      sdr[c * 128:(c + 1) * 128, :])
            s.wait_ge(P.sems["qs"], P.cnt["qs"])


def emit_saveslot(P, name="slotdram"):
    slot = P.sb("slot", [128, NCH * TOK], BF16)
    sdr = P.dram(name, [D, TOK], BF16, io="x")
    with P.block_guard() as blk:
        @blk.sync
        def _(s):
            for c in range(NCH):
                P.dma(s, "qs", sdr[c * 128:(c + 1) * 128, :],
                      slot[:, c * TOK:(c + 1) * TOK])
            s.wait_ge(P.sems["qs"], P.cnt["qs"])



def kernel(**inputs):
    shared = prep_shared(inputs)
    cosT, sinT = _rope_tables()
    cores = [prep_core(inputs, c, cosT, sinT) for c in range(8)]

    P = build_program(PHASES_ALL)
    in_maps = []
    for c in range(8):
        m = dict(shared)
        m["identb_d"] = shared["identb"]
        m["identf_d"] = shared["identf"]
        m.update(cores[c])
        in_maps.append({k: v for k, v in m.items() if k in P.dram_t})
    res = run_bass_kernel_spmd(P.nc, in_maps, core_ids=list(range(8)))
    P.close()
    outs = [np.asarray(r["y"], np.float32) for r in res.results]
    return unprep_out(outs)
